# version 09 (final)
# baseline (speedup 1.0000x reference)
import contextlib
import numpy as np
import ml_dtypes
import concourse.bass as bass
import concourse.mybir as mybir
from concourse.bass_utils import run_bass_kernel_spmd

F32 = mybir.dt.float32
BF16 = mybir.dt.bfloat16
AF = mybir.ActivationFunctionType
ALU = mybir.AluOpType
AX = mybir.AxisListType

D = 1024
S = 4096
C = D // 128
NT = S // 512
FF = 4096
H = 16
HD = 64
NE = 8
BKV = 64
NBLK = S // BKV

BF = ml_dtypes.bfloat16


class Rec:
    def __init__(self):
        self.items = []

    def op(self, eng, fn):
        self.items.append((eng, fn))


class P:
    def __init__(self, nc, ent):
        self.nc = nc
        self.ent = ent
        self.R = Rec()
        names = ["dx", "dw", "do", "pe", "act", "dve", "gp"]
        self.sh = {n: ent(nc.semaphore(n)) for n in names}
        self.sc = {n: 0 for n in names}
        self.free_at = {}

    def sbuf(self, name, shape, dtype):
        cm = self.nc.sbuf_tensor(name, shape, dtype)
        t = cm.__enter__()
        return t, cm

    def psum(self, name, shape, dtype=F32):
        cm = self.nc.psum_tensor(name, shape, dtype)
        t = cm.__enter__()
        return t, cm

    def wait(self, eng, sem, val=None):
        v = self.sc[sem] if val is None else val
        h = self.sh[sem]
        if v > 0:
            self.R.op(eng, lambda e, h=h, v=v: e.wait_ge(h, v))

    def wait_free(self, eng, tag):
        fa = self.free_at.get(tag)
        if fa is not None:
            sem, cnt = fa
            self.wait(eng, sem, cnt)

    def set_free(self, tag, sem):
        self.free_at[tag] = (sem, self.sc[sem])

    def inc(self, sem, k):
        self.sc[sem] += k
        return self.sh[sem], self.sc[sem]

    def dma(self, q, out, in_, sem="dx"):
        h, _ = self.inc(sem, 16)
        self.R.op(q, lambda e, o=out, i=in_, h=h: e.dma_start(out=o, in_=i).then_inc(h, 16))

    def section_barrier(self):
        for q in ("sync", "gpsimd"):
            for sem in ("pe", "dve", "act", "do"):
                self.wait(q, sem)

    def op(self, eng, fn, inc=None, k=1):
        if inc is None:
            self.R.op(eng, fn)
        else:
            h, _ = self.inc(inc, k)
            self.R.op(eng, lambda e, f=fn, h=h, k=k: f(e).then_inc(h, k))


def x_hbm_tile(xap, tt):
    return xap.rearrange("(c p) n -> p c n", p=128)[:, :, tt * 512:(tt + 1) * 512]


def sec_ffn(p, cst, src, dst, w1, b1r, w2, b2r, tag):
    scope = []

    def sb(name, shape, dtype):
        t, cm = p.sbuf(tag + name, shape, dtype)
        scope.append(cm)
        return t

    def ps(name, shape, dtype=F32):
        t, cm = p.psum(tag + name, shape, dtype)
        scope.append(cm)
        return t

    w1b = sb("w1b", [128, C * FF], BF16)
    w2b = sb("w2b", [128, (FF // 128) * D], BF16)
    b1t = sb("b1t", [128, FF // 128], F32)
    b2t = sb("b2t", [128, C], F32)
    xf = [sb(f"xf{i}", [128, C * 512], F32) for i in range(2)]
    wst = xf
    xb = sb("xb", [128, C * 512], BF16)
    hb = sb("hb", [128, 8 * 512], BF16)
    yac = sb("yac", [128, C * 512], F32)
    rowa = sb("rowa", [1, 512], F32)
    rowb = sb("rowb", [1, 512], F32)
    ph = [ps(f"ph{i}", [128, 512]) for i in range(2)]
    py = [ps(f"py{i}", [128, 512]) for i in range(2)]
    pls = ps("pls", [1, 512])
    plb = [ps(f"plb{i}", [128, 512]) for i in range(2)]

    p.section_barrier()
    p.wait_free("sync", "bootmem")
    p.dma("sync", b1t[:, :], b1r[:, :], "dw")
    p.dma("sync", b2t[:, :], b2r[:, :], "dw")
    for fc in range(C):
        p.wait_free("sync", tag + f"xf{fc % 2}")
        p.dma("sync", wst[fc % 2][:, :], w1[fc * 128:(fc + 1) * 128, :], "dw")
        p.wait("vector", "dw")
        p.op("vector", lambda e, fc=fc: e.tensor_copy(
            w1b[:, fc * FF:(fc + 1) * FF], wst[fc % 2][:, :]), inc="dve")
        p.set_free(tag + f"xf{fc % 2}", "dve")
    for g in range(8):
        p.wait_free("sync", tag + f"xf{g % 2}")
        p.dma("sync", wst[g % 2][:, :].rearrange("p (c n) -> p c n", n=D),
              w2[g * 512:(g + 1) * 512, :].rearrange("(c p) n -> p c n", p=128), "dw")
        p.wait("vector", "dw")
        p.op("vector", lambda e, g=g: e.tensor_copy(
            w2b[:, g * 4 * D:(g + 1) * 4 * D], wst[g % 2][:, :]), inc="dve")
        p.set_free(tag + f"xf{g % 2}", "dve")

    for tt in range(NT):
        cur = xf[tt % 2]
        p.wait_free("sync", tag + f"xf{tt % 2}")
        p.dma("sync", cur[:, :].rearrange("p (c n) -> p c n", n=512), x_hbm_tile(src, tt), "dx")
        p.wait("vector", "dx")
        p.op("vector", lambda e, cur=cur: e.tensor_copy(xb[:, :], cur[:, :]), inc="dve")
        cast_done = p.sc["dve"]

        for g in range(4):
            for hc in range(8):
                ghc = g * 8 + hc
                bank = ph[hc % 2]
                p.wait("tensor", "dve", cast_done)
                p.wait_free("tensor", tag + f"ph{hc % 2}")
                for fc in range(C):
                    lhs = w1b[:, fc * FF + ghc * 128: fc * FF + (ghc + 1) * 128]
                    rhs = xb[:, fc * 512:(fc + 1) * 512]
                    p.op("tensor", lambda e, l=lhs, r=rhs, b=bank, s=(fc == 0), t=(fc == C - 1):
                         e.matmul(b[:, :], l, r, start=s, stop=t),
                         inc="pe" if fc == C - 1 else None)
                p.wait("scalar", "pe")
                p.op("scalar", lambda e, ghc=ghc, hc=hc, b=bank: e.activation(
                    hb[:, hc * 512:(hc + 1) * 512], b[:, :], AF.Gelu,
                    bias=b1t[:, ghc:ghc + 1], scale=1.0), inc="act")
                p.set_free(tag + f"ph{hc % 2}", "act")
            gelu_done = p.sc["act"]
            for ec in range(C):
                bank = py[ec % 2]
                p.wait("tensor", "act", gelu_done)
                p.wait_free("tensor", tag + f"py{ec % 2}")
                for hc in range(8):
                    ghc = g * 8 + hc
                    lhs = w2b[:, ghc * D + ec * 128: ghc * D + (ec + 1) * 128]
                    rhs = hb[:, hc * 512:(hc + 1) * 512]
                    p.op("tensor", lambda e, l=lhs, r=rhs, b=bank, s=(hc == 0), t=(hc == 7):
                         e.matmul(b[:, :], l, r, start=s, stop=t),
                         inc="pe" if hc == 7 else None)
                yslice = yac[:, ec * 512:(ec + 1) * 512]
                if g == 0:
                    p.wait("scalar", "pe")
                    p.op("scalar", lambda e, y=yslice, b=bank, ec=ec: e.activation(
                        y, b[:, :], AF.Identity, bias=b2t[:, ec:ec + 1], scale=1.0), inc="act")
                    p.set_free(tag + f"py{ec % 2}", "act")
                else:
                    p.wait("vector", "pe")
                    p.op("vector", lambda e, y=yslice, b=bank: e.tensor_tensor(
                        y, y, b[:, :], ALU.add), inc="dve")
                    p.set_free(tag + f"py{ec % 2}", "dve")
        ydone_act = p.sc["act"]
        p.wait("vector", "act", ydone_act)
        for oc in range(C):
            p.op("vector", lambda e, oc=oc, cur=cur: e.tensor_tensor(
                yac[:, oc * 512:(oc + 1) * 512], yac[:, oc * 512:(oc + 1) * 512],
                cur[:, oc * 512:(oc + 1) * 512], ALU.add), inc="dve")
        ln_epilogue(p, cst, tag + f"xf{tt % 2}|" + tag, tt, yac, xb, hb, cur,
                    rowa, rowb, pls, plb, dst)

    for cm in reversed(scope):
        cm.__exit__(None, None, None)


def ln_epilogue(p, cst, tag, tt, yac, zbuf, sqbuf, xo, rowa, rowb, pls, plb, dst,
                extra_out=None, dbgrow=None):
    if "|" in tag:
        xo_tag, tag = tag.split("|")
    else:
        xo_tag = tag + "xo"
    """LayerNorm over features of yac (fp32 [128, C*512]); writes fp32 tile to
    dst HBM (feature-major). zbuf: bf16 scratch [128, C*512]; sqbuf: bf16
    scratch (>= [128, C*512]). Trashes yac (centers in place)."""
    p.op("vector", lambda e: e.tensor_copy(zbuf[:, 0:C * 512], yac[:, :]), inc="dve")
    zb_done = p.sc["dve"]
    p.wait("tensor", "dve", zb_done)
    p.wait_free("tensor", tag + "pls")
    for oc in range(C):
        p.op("tensor", lambda e, oc=oc, s=(oc == 0), t=(oc == C - 1): e.matmul(
            pls[:, :], cst["ones_col_bf"][:, :], zbuf[:, oc * 512:(oc + 1) * 512],
            start=s, stop=t), inc="pe" if oc == C - 1 else None)
    p.wait("scalar", "pe")
    p.op("scalar", lambda e: e.activation(rowa[:, :], pls[:, :], AF.Identity,
                                          scale=-1.0 / D), inc="act")
    mean_done = p.sc["act"]
    p.wait("tensor", "act", mean_done)
    p.wait_free("tensor", tag + "plb0")
    p.op("tensor", lambda e: e.matmul(plb[0][:, :], cst["ones_row_f"][:, :],
                                      rowa[:, :], start=True, stop=True), inc="pe")
    p.wait("vector", "pe")
    for oc in range(C):
        p.op("vector", lambda e, oc=oc: e.tensor_tensor(
            yac[:, oc * 512:(oc + 1) * 512], yac[:, oc * 512:(oc + 1) * 512],
            plb[0][:, :], ALU.add), inc="dve")
    p.set_free(tag + "plb0", "dve")
    p.wait("scalar", "dve")
    for oc in range(C):
        p.op("scalar", lambda e, oc=oc: e.activation(
            sqbuf[:, oc * 512:(oc + 1) * 512], yac[:, oc * 512:(oc + 1) * 512],
            AF.Square), inc="act")
    p.wait("tensor", "act")
    for oc in range(C):
        p.op("tensor", lambda e, oc=oc, s=(oc == 0), t=(oc == C - 1): e.matmul(
            pls[:, :], cst["ones_col_bf"][:, :], sqbuf[:, oc * 512:(oc + 1) * 512],
            start=s, stop=t), inc="pe" if oc == C - 1 else None)
    p.wait("scalar", "pe")
    p.op("scalar", lambda e: e.activation(rowb[:, :], pls[:, :], AF.Sqrt,
                                          bias=cst["eps1"][:, :], scale=1.0 / D), inc="act")
    p.set_free(tag + "pls", "act")
    p.wait("vector", "act")
    p.op("vector", lambda e: e.reciprocal(rowa[:, :], rowb[:, :]), inc="dve")
    p.wait("tensor", "dve")
    p.wait_free("tensor", tag + "plb1")
    p.op("tensor", lambda e: e.matmul(plb[1][:, :], cst["ones_row_f"][:, :],
                                      rowa[:, :], start=True, stop=True), inc="pe")
    p.wait("vector", "pe")
    for oc in range(C):
        p.op("vector", lambda e, oc=oc: e.tensor_tensor(
            xo[:, 0:C * 512][:, oc * 512:(oc + 1) * 512], yac[:, oc * 512:(oc + 1) * 512],
            plb[1][:, :], ALU.mult), inc="dve")
    p.set_free(tag + "plb1", "dve")
    p.wait("gpsimd", "dve")
    if dbgrow is not None:
        p.dma("gpsimd", dbgrow[0][0:1, 0:512], rowa[:, :], "do")
        p.dma("gpsimd", dbgrow[0][1:2, 0:512], rowb[:, :], "do")
        p.dma("gpsimd", dbgrow[0][2:34, 0:512], zbuf[0:32, 0:512], "do")
        p.dma("gpsimd", dbgrow[0][34:66, 0:512], xo[0:32, 0:512], "do")
        p.dma("gpsimd", dbgrow[0][66:98, 0:512], yac[0:32, 0:512], "do")
        for engx in ("vector", "scalar", "tensor", "gpsimd", "sync"):
            p.wait(engx, "do")
    p.dma("gpsimd", x_hbm_tile(dst, tt), xo[:, 0:C * 512].rearrange("p (c n) -> p c n", n=512), "do")
    if extra_out is not None:
        p.dma("gpsimd", x_hbm_tile(extra_out, tt),
              xo[:, 0:C * 512].rearrange("p (c n) -> p c n", n=512), "do")
    p.set_free(xo_tag, "do")


def build_consts(p):
    nc = p.nc
    cst = {}
    specs = [
        ("ones_col_bf", [128, 1], BF16),
        ("ones_row_f", [1, 128], F32),
        ("ident_bf", [128, 128], BF16),
        ("onesA_bf", [128, 1], BF16),
        ("onesB_bf", [128, 1], BF16),
        ("onesA_f", [1, 128], F32),
        ("onesB_f", [1, 128], F32),
        ("mask_bf", [128, 2048], BF16),
        ("ident_f", [128, 128], F32),
        ("sel8", [8, 1024], BF16),
    ]
    for name, shape, dt in specs:
        a = nc.dram_tensor("c_" + name, shape, F32, kind="ExternalInput").ap()
        t, _ = p.sbuf("cst_" + name, shape, dt)
        p.dma("gpsimd", t[:, :], a[:, :], "dw")
        cst[name] = t
    t3, _ = p.sbuf("cst_eps1", [1, 1], F32)
    p.op("vector", lambda e: e.memset(t3[:, :], 1e-5), inc="dve")
    cst["eps1"] = t3
    return cst


def const_arrays():
    i = np.arange(128)
    maskv = np.zeros((128, 2048), np.float32)
    for v in range(4):
        j = np.arange(512)
        maskv[:, v * 512:(v + 1) * 512] = (j[None, :] <= v * 128 + i[:, None])
    onesA = np.zeros((128, 1), np.float32); onesA[0:64] = 1.0
    onesB = np.zeros((128, 1), np.float32); onesB[64:128] = 1.0
    return {
        "c_ones_col_bf": np.ones((128, 1), np.float32),
        "c_ones_row_f": np.ones((1, 128), np.float32),
        "c_ident_bf": np.eye(128, dtype=np.float32),
        "c_onesA_bf": onesA, "c_onesB_bf": onesB,
        "c_onesA_f": onesA.T.copy(), "c_onesB_f": onesB.T.copy(),
        "c_mask_bf": maskv,
        "c_ident_f": np.eye(128, dtype=np.float32),
        "c_sel8": np.concatenate([np.eye(8, dtype=np.float32)[:, e:e+1].repeat(128, 1) for e in range(8)], axis=1),
    }


def emit(nc, p):
    with nc.Block() as block:
        for engname in ["sync", "tensor", "vector", "scalar", "gpsimd"]:
            items = [fn for en, fn in p.R.items if en == engname]
            if not items:
                continue
            dec = getattr(block, engname)

            def body(eng, _items=items):
                for fn in _items:
                    fn(eng)
            dec(body)


def build_ffn_only():
    nc = bass.Bass("TRN2", target_bir_lowering=False, debug=False)
    xin = nc.dram_tensor("xin", [D, S], F32, kind="ExternalInput").ap()
    xout = nc.dram_tensor("xout", [D, S], F32, kind="ExternalOutput").ap()
    w1 = nc.dram_tensor("w1", [D, FF], F32, kind="ExternalInput").ap()
    b1r = nc.dram_tensor("b1r", [128, FF // 128], F32, kind="ExternalInput").ap()
    w2 = nc.dram_tensor("w2", [FF, D], F32, kind="ExternalInput").ap()
    b2r = nc.dram_tensor("b2r", [128, C], F32, kind="ExternalInput").ap()
    ctx = contextlib.ExitStack()
    with ctx:
        p = P(nc, ctx.enter_context)
        cst = build_consts(p)
        sec_ffn(p, cst, xin, xout, w1, b1r, w2, b2r, "f0_")
        p.wait("sync", "do", p.sc["do"])
        emit(nc, p)
    return nc


_NC_CACHE = {}


def build_full():
    nc = bass.Bass("TRN2", target_bir_lowering=False, debug=False)
    ap = lambda name, shape, dt=F32, kind="ExternalInput": nc.dram_tensor(name, shape, dt, kind=kind).ap()
    xin = ap("xin", [D, S])
    xout = ap("xout", [D, S], F32, "ExternalOutput")
    wq0, wk0, wv0, wo0 = (ap(n, [D, D]) for n in ("wq0", "wk0", "wv0", "wo0"))
    bor0 = ap("bor0", [128, C])
    w1 = ap("w1", [D, FF]); b1r = ap("b1r", [128, 32])
    w2 = ap("w2", [FF, D]); b2r = ap("b2r", [128, C])
    wq1, wk1, wv1, wo1 = (ap(n, [D, D]) for n in ("wq1", "wk1", "wv1", "wo1"))
    wgT = ap("wgT", [128, 64])
    W1 = [ap(f"W1_{e}", [D, FF]) for e in range(NE)]
    W2 = [ap(f"W2_{e}", [FF, D]) for e in range(NE)]
    b1all = ap("b1all", [128, 32 * NE])
    b2all = ap("b2all", [128, 8 * NE])
    qs = ap("qs", [D, S], BF16, "Internal")
    ks = ap("ks", [D, S], BF16, "Internal")
    vs = ap("vs", [D, S], BF16, "Internal")
    xs_a = ap("xs_a", [D, S], F32, "Internal")
    xs_b = ap("xs_b", [D, S], F32, "Internal")
    xs_c = ap("xs_c", [D, S], F32, "Internal")
    macc = ap("macc", [D, S], F32, "Internal")
    ctx = contextlib.ExitStack()
    with ctx:
        p = P(nc, ctx.enter_context)
        cst = build_consts(p)
        sec_proj(p, cst, xin, [qs, ks, vs], [wq0, wk0, wv0], "pj0_")
        sec_attn0(p, cst, xin, xs_a, qs, ks, vs, wo0, bor0, "a0_")
        sec_ffn(p, cst, xs_a, xs_b, w1, b1r, w2, b2r, "f0_")
        sec_proj(p, cst, xs_b, [qs, ks, vs], [wq1, wk1, wv1], "pj1_")
        sec_attn1(p, cst, xs_b, xs_c, qs, ks, vs, wo1, "a1_")
        sec_moe(p, cst, xs_c, xout, wgT, W1, b1all, W2, b2all, macc, "m1_")
        p.wait("sync", "do", p.sc["do"])
        emit(nc, p)
    return nc


def kernel(L0_attn_x, L0_attn_Wq, L0_attn_Wk, L0_attn_Wv, L0_attn_Wo, L0_attn_bo,
           L0_ffn_w1, L0_ffn_b1, L0_ffn_w2, L0_ffn_b2,
           L1_attn_Wq, L1_attn_Wk, L1_attn_Wv, L1_attn_Wo,
           L1_ffn_w_gate, L1_ffn_W1, L1_ffn_b1, L1_ffn_W2, L1_ffn_b2):
    x = np.asarray(L0_attn_x, dtype=np.float32)
    B = x.shape[0]
    if "nc" not in _NC_CACHE:
        _NC_CACHE["nc"] = build_full()
    nc = _NC_CACHE["nc"]
    f = np.float32
    shared = {
        "wq0": np.ascontiguousarray(L0_attn_Wq, f), "wk0": np.ascontiguousarray(L0_attn_Wk, f),
        "wv0": np.ascontiguousarray(L0_attn_Wv, f), "wo0": np.ascontiguousarray(L0_attn_Wo, f),
        "bor0": np.ascontiguousarray(np.asarray(L0_attn_bo, f).reshape(8, 128).T),
        "w1": np.ascontiguousarray(L0_ffn_w1, f),
        "b1r": np.ascontiguousarray(np.asarray(L0_ffn_b1, f).reshape(32, 128).T),
        "w2": np.ascontiguousarray(L0_ffn_w2, f),
        "b2r": np.ascontiguousarray(np.asarray(L0_ffn_b2, f).reshape(8, 128).T),
        "wq1": np.ascontiguousarray(L1_attn_Wq, f), "wk1": np.ascontiguousarray(L1_attn_Wk, f),
        "wv1": np.ascontiguousarray(L1_attn_Wv, f), "wo1": np.ascontiguousarray(L1_attn_Wo, f),
        "wgT": np.ascontiguousarray(
            np.asarray(L1_ffn_w_gate, f).T.reshape(8, 128, 8).transpose(1, 0, 2).reshape(128, 64)),
        **const_arrays(),
    }
    W1a = np.asarray(L1_ffn_W1, f); W2a = np.asarray(L1_ffn_W2, f)
    b1a = np.asarray(L1_ffn_b1, f); b2a = np.asarray(L1_ffn_b2, f)
    for e in range(NE):
        shared[f"W1_{e}"] = np.ascontiguousarray(W1a[e])
        shared[f"W2_{e}"] = np.ascontiguousarray(W2a[e])
    b1all = np.zeros((128, NE * 32), f)
    b2all = np.zeros((128, NE * 8), f)
    for e in range(NE):
        b1all[:, e * 32:(e + 1) * 32] = b1a[e].reshape(32, 128).T
        b2all[:, e * 8:(e + 1) * 8] = b2a[e].reshape(8, 128).T
    shared["b1all"] = b1all
    shared["b2all"] = b2all
    in_maps = []
    for c in range(B):
        m = dict(shared)
        m["xin"] = np.ascontiguousarray(x[c].T)
        in_maps.append(m)
    res = run_bass_kernel_spmd(nc, in_maps, core_ids=list(range(B)))
    out = np.stack([np.asarray(r["xout"]).T for r in res.results], axis=0)
    return out.astype(np.float32)


def sec_proj(p, cst, src, outs, ws, tag):
    scope = []

    def sb(name, shape, dtype):
        t, cm = p.sbuf(tag + name, shape, dtype)
        scope.append(cm)
        return t

    def ps(name, shape, dtype=F32):
        t, cm = p.psum(tag + name, shape, dtype)
        scope.append(cm)
        return t

    xball = sb("xball", [128, C * S], BF16)
    wb = sb("wb", [128, C * D], BF16)
    wstg = [sb(f"wstg{i}", [128, D], F32) for i in range(2)]
    xstg = [sb(f"xstg{i}", [128, S], F32) for i in range(2)]
    obuf = [sb(f"obuf{i}", [128, S], BF16) for i in range(2)]
    pp = [ps(f"pp{i}", [128, 512]) for i in range(2)]

    p.section_barrier()
    p.wait_free("sync", "bootmem")
    for fc in range(C):
        p.wait_free("sync", tag + f"xstg{fc % 2}")
        p.dma("sync", xstg[fc % 2][:, :], src[fc * 128:(fc + 1) * 128, :], "dx")
        p.wait("vector", "dx")
        p.op("vector", lambda e, fc=fc: e.tensor_copy(
            xball[:, fc * S:(fc + 1) * S], xstg[fc % 2][:, :]), inc="dve")
        p.set_free(tag + f"xstg{fc % 2}", "dve")

    for wi, (w_ap, out_ap) in enumerate(zip(ws, outs)):
        p.wait_free("vector", tag + "wb")
        for fc in range(C):
            p.wait_free("sync", tag + f"wstg{fc % 2}")
            p.dma("sync", wstg[fc % 2][:, :], w_ap[fc * 128:(fc + 1) * 128, :], "dw")
            p.wait("vector", "dw")
            p.op("vector", lambda e, fc=fc: e.tensor_copy(
                wb[:, fc * D:(fc + 1) * D], wstg[fc % 2][:, :]), inc="dve")
            p.set_free(tag + f"wstg{fc % 2}", "dve")
        wcast = p.sc["dve"]
        for oc in range(C):
            ob = obuf[oc % 2]
            p.wait_free("scalar", tag + f"obuf{oc % 2}")
            for tt in range(NT):
                bank = pp[tt % 2]
                p.wait("tensor", "dve", wcast)
                p.wait_free("tensor", tag + f"pp{tt % 2}")
                for fc in range(C):
                    lhs = wb[:, fc * D + oc * 128: fc * D + (oc + 1) * 128]
                    rhs = xball[:, fc * S + tt * 512: fc * S + (tt + 1) * 512]
                    p.op("tensor", lambda e, l=lhs, r=rhs, b=bank, s=(fc == 0), t=(fc == C - 1):
                         e.matmul(b[:, :], l, r, start=s, stop=t),
                         inc="pe" if fc == C - 1 else None)
                p.wait("scalar", "pe")
                p.op("scalar", lambda e, b=bank, tt=tt, ob=ob: e.activation(
                    ob[:, tt * 512:(tt + 1) * 512], b[:, :], AF.Copy), inc="act")
                p.set_free(tag + f"pp{tt % 2}", "act")
            p.wait("gpsimd", "act")
            p.dma("gpsimd", out_ap[oc * 128:(oc + 1) * 128, :], ob[:, :], "do")
            p.set_free(tag + f"obuf{oc % 2}", "do")
        p.set_free(tag + "wb", "pe")

    for cm in reversed(scope):
        cm.__exit__(None, None, None)


def sec_attn0(p, cst, src, dst, qs, ks, vs, wo, bor, tag, extra_out=None, dbg=None, only_hp0=False, skip_flash=False):
    scope = []

    def sb(name, shape, dtype):
        t, cm = p.sbuf(tag + name, shape, dtype)
        scope.append(cm)
        return t

    ctxT = sb("ctxT", [128, C * S], BF16)

    scope_b = []

    def sbb(name, shape, dtype):
        t, cm = p.sbuf(tag + name, shape, dtype)
        scope_b.append(cm)
        return t

    def psb(name, shape, dtype=F32):
        t, cm = p.psum(tag + name, shape, dtype)
        scope_b.append(cm)
        return t

    q2 = sbb("q2", [128, S], BF16)
    k2 = sbb("k2", [128, S], BF16)
    v2 = sbb("v2", [128, S], BF16)
    vte = sbb("vte", [128, 2 * 32 * 65], BF16)
    eA = [sbb("eA0", [128, S], BF16), sbb("eA1", [128, S], BF16)]
    eB = [sbb("eB0", [128, S], BF16), sbb("eB1", [128, S], BF16)]
    ptA = sbb("ptA", [128, 32 * 512], BF16)
    ptB = sbb("ptB", [128, 32 * 512], BF16)
    sq2 = eA[0]
    rowr = sbb("rowr", [1, 512], F32)
    rowr2 = sbb("rowr2", [1, 512], F32)
    smf = sbb("smf", [1, 8], F32)
    smf2 = sbb("smf2", [1, 8], F32)
    nbt = sbb("nbt", [128, 2], F32)
    rbt = sbb("rbt", [128, 512], BF16)
    cxt = sbb("cxt", [128, 512], BF16)
    psS = [[psb("psSA0", [128, 512]), psb("psSA1", [128, 512])],
           [psb("psSB0", [128, 512]), psb("psSB1", [128, 512])]]
    pvv = [psb("pvA", [65, 512]), psb("pvB", [65, 512])]
    pmis = psS[0][0]
    ptp = [psb(f"ptp{i}", [128, 1024], BF16) for i in range(2)]

    p.section_barrier()
    p.wait_free("sync", "bootmem")
    for hp in range(C):
        p.wait_free("sync", tag + "qkv")
        for (t_, s_) in ((q2, qs), (k2, ks), (v2, vs)):
            p.dma("sync", t_[:, :], s_[hp * 128:(hp + 1) * 128, :], "dx")
        ld_qkv = p.sc["dx"]

        p.wait("vector", "dx", ld_qkv)
        p.wait_free("vector", tag + "vte")
        p.op("vector", lambda e: e.memset(vte[:, :], 1.0), inc="dve")
        vone = p.sc["dve"]
        for head in range(2):
            for kcg in range(4):
                bank = ptp[kcg % 2]
                p.wait("tensor", "dve", vone)
                p.wait_free("tensor", tag + f"ptp{kcg % 2}")
                ide = cst["ident_bf"][head * 64:head * 64 + 64, head * 64:head * 64 + 64]
                for j in range(8):
                    kc = kcg * 8 + j
                    vsl = v2[head * 64:head * 64 + 64, kc * 128:(kc + 1) * 128]
                    p.op("tensor", lambda e, v=vsl, b=bank, j=j, ide=ide: e.transpose(
                        b[:, j * 64:(j + 1) * 64], v, ide),
                        inc="pe" if j == 7 else None)
                p.wait("vector", "pe")
                dstv = vte[:, head * 2080 + kcg * 8 * 65: head * 2080 + (kcg * 8 + 8) * 65]
                p.op("vector", lambda e, b=bank, d=dstv: e.tensor_copy(
                    d.rearrange("p (c n) -> p c n", n=65)[:, :, 0:64],
                    b[:, 0:512].rearrange("p (c n) -> p c n", n=64)), inc="dve")
                p.set_free(tag + f"ptp{kcg % 2}", "dve")
        vte_done = p.sc["dve"]

        for which, t_ in ((0, q2), (1, k2)):
            p.wait("scalar", "dx", ld_qkv)
            if which == 1:
                p.wait("scalar", "pe")
            p.op("scalar", lambda e, t_=t_: e.activation(sq2[:, :], t_[:, :], AF.Square),
                 inc="act")
            sqd = p.sc["act"]
            for head in range(2):
                onescol = cst["onesA_bf"] if head == 0 else cst["onesB_bf"]
                for tt in range(NT):
                    p.wait("tensor", "act", sqd)
                    p.wait_free("tensor", tag + "pmis")
                    p.op("tensor", lambda e, o=onescol, tt=tt: e.matmul(
                        pmis[0:1, :], o[:, :], sq2[:, tt * 512:(tt + 1) * 512],
                        start=True, stop=True), inc="pe")
                    p.wait("vector", "pe")
                    p.op("vector", lambda e, tt=tt: e.reduce_max(
                        smf2[:, tt:tt + 1], pmis[0:1, :], axis=AX.X), inc="dve")
                    p.set_free(tag + "pmis", "dve"); p.set_free(tag + "psS00", "dve")
                p.wait("vector", "dve")
                p.op("vector", lambda e, i=which * 2 + head: e.reduce_max(
                    smf[:, i:i + 1], smf2[:, :], axis=AX.X), inc="dve")
        for head in range(2):
            p.wait("vector", "dve")
            p.op("vector", lambda e, h=head: e.tensor_tensor(
                smf[:, 4 + h:5 + h], smf[:, h:h + 1], smf[:, 2 + h:3 + h], ALU.mult),
                inc="dve")
            p.wait("scalar", "dve")
            p.op("scalar", lambda e, h=head: e.activation(
                smf[:, 6 + h:7 + h], smf[:, 4 + h:5 + h], AF.Sqrt), inc="act")
            p.wait("vector", "act")
            p.op("vector", lambda e, h=head: e.tensor_scalar(
                smf[:, 4 + h:5 + h], smf[:, 6 + h:7 + h], -0.125, None, ALU.mult),
                inc="dve")
            p.wait("tensor", "dve")
            p.wait_free("tensor", tag + "pmis")
            p.op("tensor", lambda e, h=head: e.matmul(
                pmis[:, 0:1], cst["ones_row_f"][:, :], smf[:, 4 + h:5 + h],
                start=True, stop=True), inc="pe")
            p.wait("scalar", "pe")
            p.op("scalar", lambda e, h=head: e.activation(
                nbt[:, h:h + 1], pmis[:, 0:1], AF.Copy), inc="act")
            p.set_free(tag + "pmis", "act"); p.set_free(tag + "psS00", "act")
        nb_done = p.sc["act"]
        if dbg is not None and len(dbg) > 4 and hp == 0:
            p.wait("gpsimd", "act", nb_done)
            p.wait("gpsimd", "dve")
            p.dma("gpsimd", dbg[4][:, 0:2], nbt[:, :], "do")
            p.dma("gpsimd", dbg[4][0:1, 2:10], smf[:, :], "do")
            p.dma("gpsimd", dbg[4][0:1, 10:18], smf2[:, :], "do")
            p.dma("gpsimd", dbg[3][:, :], sq2[:, :].rearrange("p n -> p n"), "do")
        if skip_flash:
            p.set_free(tag + "qkv", "pe")
            p.set_free(tag + "vte", "pe")
            break

        def s_exp0(qt):
            qg = qt // 4
            qb = qt % 2
            for kt in range(qg + 1):
                w = max(nb_done,
                        p.free_at.get(tag + f"psS0{kt % 2}", ("act", 0))[1],
                        p.free_at.get(tag + f"psS1{kt % 2}", ("act", 0))[1])
                p.wait("tensor", "act", w)
                p.wait_free("tensor", tag + "pmis")
                if kt == 0:
                    p.wait_free("tensor", tag + f"e0{qb}")
                    p.wait_free("tensor", tag + f"e1{qb}")
                for head in (0, 1):
                    ps_ = psS[head][kt % 2]
                    lhs = q2[head * 64:head * 64 + 64, qt * 128:(qt + 1) * 128]
                    rhs = k2[head * 64:head * 64 + 64, kt * 512:(kt + 1) * 512]
                    p.op("tensor", lambda e, l=lhs, r=rhs, b=ps_: e.matmul(
                        b[:, :], l, r, start=True, stop=True),
                        inc="pe" if head == 1 else None)
                p.wait("scalar", "pe")
                for head, e_ in ((0, eA[qb]), (1, eB[qb])):
                    ps_ = psS[head][kt % 2]
                    p.op("scalar", lambda e, b=ps_, e_=e_, kt=kt, h=head: e.activation(
                        e_[:, kt * 512:(kt + 1) * 512], b[:, :], AF.Exp,
                        bias=nbt[:, h:h + 1], scale=0.125), inc="act")
                p.set_free(tag + f"psS0{kt % 2}", "act")
                p.set_free(tag + f"psS1{kt % 2}", "act")

        def mask_tr0(qt, exp_done):
            qg = qt // 4
            qtm = qt % 4
            qb = qt % 2
            p.wait("vector", "act", exp_done)
            for e_ in (eA[qb], eB[qb]):
                p.op("vector", lambda e, e_=e_, qg=qg, qtm=qtm: e.tensor_tensor(
                    e_[:, qg * 512:(qg + 1) * 512], e_[:, qg * 512:(qg + 1) * 512],
                    cst["mask_bf"][:, qtm * 512:(qtm + 1) * 512], ALU.mult), inc="dve")
            mask_done = p.sc["dve"]
            nkc = 4 * qg + 4
            for head, e_, pt_ in ((0, eA[qb], ptA), (1, eB[qb], ptB)):
                for kcg in range((nkc + 7) // 8):
                    bank = ptp[kcg % 2]
                    j0 = kcg * 8
                    j1 = min(j0 + 8, nkc)
                    p.wait("tensor", "dve", mask_done)
                    p.wait_free("tensor", tag + f"ptp{kcg % 2}")
                    p.wait_free("tensor", tag + "ptAB")
                    for j in range(j0, j1):
                        p.op("tensor", lambda e, e_=e_, b=bank, j=j, j0=j0: e.transpose(
                            b[:, (j - j0) * 128:(j - j0 + 1) * 128],
                            e_[:, j * 128:(j + 1) * 128], cst["ident_bf"][:, :]),
                            inc="pe" if j == j1 - 1 else None)
                    p.wait("scalar", "pe")
                    nj = j1 - j0
                    dsl = pt_[:, qtm * S + j0 * 128: qtm * S + j1 * 128]
                    p.op("scalar", lambda e, b=bank, d=dsl, nj=nj: e.activation(
                        d, b[:, 0:nj * 128], AF.Copy), inc="act")
                    p.set_free(tag + f"ptp{kcg % 2}", "act")
                p.set_free(tag + f"e{head}{qb}", "pe")

        def av0(qg):
            pt_done = p.sc["act"]
            nkc = 4 * qg + 4
            for head, pt_, pv_ in ((0, ptA, pvv[0]), (1, ptB, pvv[1])):
                p.wait("tensor", "act", pt_done)
                p.wait_free("tensor", tag + f"pv{head}")
                pt3 = pt_[:, :].rearrange("p (m n) -> p m n", n=128)
                for kc in range(nkc):
                    lhs = vte[:, head * 2080 + kc * 65: head * 2080 + (kc + 1) * 65]
                    rhs = pt3[:, kc::32, :]
                    p.op("tensor", lambda e, l=lhs, r=rhs, b=pv_, s=(kc == 0),
                         t=(kc == nkc - 1): e.matmul(b[:, :], l, r, start=s, stop=t),
                         inc="pe" if kc == nkc - 1 else None)
            p.set_free(tag + "ptAB", "pe")
            av_done = p.sc["pe"]
            p.wait("scalar", "pe", av_done)
            p.op("scalar", lambda e: e.activation(rowr[:, :], pvv[0][64:65, :], AF.Ln),
                 inc="act")
            p.op("scalar", lambda e: e.activation(rowr2[:, :], rowr[:, :], AF.Exp,
                                                  scale=-1.0), inc="act")
            p.wait("tensor", "act")
            p.wait_free("tensor", tag + "pmis")
            p.wait_free("tensor", tag + "psS00")
            p.op("tensor", lambda e: e.matmul(pmis[:, :], cst["onesA_f"][:, :],
                                              rowr2[:, :], start=True, stop=False), inc="pe")
            p.op("scalar", lambda e: e.activation(rowr[:, :], pvv[1][64:65, :], AF.Ln),
                 inc="act")
            p.wait("scalar", "pe")
            p.op("scalar", lambda e: e.activation(rowr2[:, :], rowr[:, :], AF.Exp,
                                                  scale=-1.0), inc="act")
            p.wait("tensor", "act")
            p.op("tensor", lambda e: e.matmul(pmis[:, :], cst["onesB_f"][:, :],
                                              rowr2[:, :], start=False, stop=True), inc="pe")
            p.wait("scalar", "pe")
            p.wait("scalar", "dve")
            p.op("scalar", lambda e: e.activation(rbt[:, :], pmis[:, :], AF.Copy), inc="act")
            p.set_free(tag + "pmis", "act"); p.set_free(tag + "psS00", "act")
            p.op("scalar", lambda e: e.activation(cxt[0:64, :], pvv[0][0:64, :], AF.Copy),
                 inc="act")
            p.op("scalar", lambda e: e.activation(cxt[64:128, :], pvv[1][0:64, :], AF.Copy),
                 inc="act")
            p.set_free(tag + "pv0", "act")
            p.set_free(tag + "pv1", "act")
            p.wait("vector", "act")
            p.op("vector", lambda e, hp=hp, qg=qg: e.tensor_tensor(
                ctxT[:, hp * S + qg * 512: hp * S + (qg + 1) * 512], cxt[:, :],
                rbt[:, :], ALU.mult), inc="dve")

        exp_dones0 = {}
        for step in range(33):
            if step < 32:
                s_exp0(step)
                exp_dones0[step] = p.sc["act"]
            if step >= 1:
                mask_tr0(step - 1, exp_dones0.pop(step - 1))
            if step >= 4 and step % 4 == 0:
                av0(step // 4 - 1)
        p.set_free(tag + "qkv", "pe")
        p.set_free(tag + "vte", "pe")
        if only_hp0:
            break

    if dbg is not None:
        p.wait("gpsimd", "dve")
        for hp_ in range(C):
            p.dma("gpsimd", dbg[1][:, hp_ * 512:(hp_ + 1) * 512],
                  ctxT[:, hp_ * S: hp_ * S + 512].rearrange("p n -> p n"), "do")
        for engx in ("vector", "scalar", "tensor", "gpsimd", "sync"):
            p.wait(engx, "do")

    for cm in reversed(scope_b):
        cm.__exit__(None, None, None)

    outproj_ln(p, cst, ctxT, src, dst, wo, bor, tag, extra_out=extra_out, dbg=dbg)

    for cm in reversed(scope):
        cm.__exit__(None, None, None)


def outproj_ln(p, cst, ctxT, src, dst, wo, bor, tag, extra_out=None, dbg=None):
    scope = []
    p.section_barrier()

    def sb(name, shape, dtype):
        t, cm = p.sbuf(tag + name, shape, dtype)
        scope.append(cm)
        return t

    def ps(name, shape, dtype=F32):
        t, cm = p.psum(tag + name, shape, dtype)
        scope.append(cm)
        return t

    wob = sb("wob", [128, C * D], BF16)
    bot = sb("bot", [128, C], F32)
    xf = [sb(f"oxf{i}", [128, C * 512], F32) for i in range(2)]
    xb = sb("oxb", [128, C * 512], BF16)
    hb = sb("ohb", [128, C * 512], BF16)
    yac = sb("oyac", [128, C * 512], F32)
    rowa = sb("orowa", [1, 512], F32)
    rowb = sb("orowb", [1, 512], F32)
    po = [ps(f"po{i}", [128, 512]) for i in range(2)]
    pls = ps("opls", [1, 512])
    plb = [ps(f"oplb{i}", [128, 512]) for i in range(2)]

    if bor is not None:
        p.dma("sync", bot[:, :], bor[:, :], "dw")
    for fc in range(C):
        p.wait_free("sync", tag + f"oxf{fc % 2}")
        p.dma("sync", xf[fc % 2][:, 0:D], wo[fc * 128:(fc + 1) * 128, :], "dw")
        p.wait("vector", "dw")
        p.op("vector", lambda e, fc=fc: e.tensor_copy(
            wob[:, fc * D:(fc + 1) * D], xf[fc % 2][:, 0:D]), inc="dve")
        p.set_free(tag + f"oxf{fc % 2}", "dve")
    wcast = p.sc["dve"]

    for tt in range(NT):
        cur = xf[tt % 2]
        p.wait_free("sync", tag + f"oxf{tt % 2}")
        p.dma("sync", cur[:, :].rearrange("p (c n) -> p c n", n=512), x_hbm_tile(src, tt), "dx")
        xld = p.sc["dx"]
        for oc in range(C):
            bank = po[oc % 2]
            p.wait("tensor", "dve", wcast)
            p.wait_free("tensor", tag + f"po{oc % 2}")
            for hp in range(C):
                lhs = wob[:, hp * D + oc * 128: hp * D + (oc + 1) * 128]
                rhs = ctxT[:, hp * S + tt * 512: hp * S + (tt + 1) * 512]
                p.op("tensor", lambda e, l=lhs, r=rhs, b=bank, s=(hp == 0), t=(hp == C - 1):
                     e.matmul(b[:, :], l, r, start=s, stop=t),
                     inc="pe" if hp == C - 1 else None)
            p.wait("scalar", "pe")
            if bor is not None:
                p.op("scalar", lambda e, b=bank, oc=oc: e.activation(
                    yac[:, oc * 512:(oc + 1) * 512], b[:, :], AF.Identity,
                    bias=bot[:, oc:oc + 1], scale=1.0), inc="act")
            else:
                p.op("scalar", lambda e, b=bank, oc=oc: e.activation(
                    yac[:, oc * 512:(oc + 1) * 512], b[:, :], AF.Copy), inc="act")
            p.set_free(tag + f"po{oc % 2}", "act")
        if dbg is not None and tt == 0:
            p.wait("gpsimd", "act")
            p.dma("gpsimd", dbg[0][:, :], yac[:, :].rearrange("p n -> p n"), "do")
            for engx in ("vector", "scalar", "tensor", "gpsimd", "sync"):
                p.wait(engx, "do")
        p.wait("vector", "act")
        p.wait("vector", "dx", xld)
        for oc in range(C):
            p.op("vector", lambda e, oc=oc, cur=cur: e.tensor_tensor(
                yac[:, oc * 512:(oc + 1) * 512], yac[:, oc * 512:(oc + 1) * 512],
                cur[:, oc * 512:(oc + 1) * 512], ALU.add), inc="dve")
        ln_epilogue(p, cst, tag + f"oxf{tt % 2}|" + tag + "o", tt, yac, xb, hb, cur,
                    rowa, rowb, pls, plb, dst, extra_out=extra_out)

    for cm in reversed(scope):
        cm.__exit__(None, None, None)


def build_attn0_only():
    nc = bass.Bass("TRN2", target_bir_lowering=False, debug=False)
    xin = nc.dram_tensor("xin", [D, S], F32, kind="ExternalInput").ap()
    xout = nc.dram_tensor("xout", [D, S], F32, kind="ExternalOutput").ap()
    wq = nc.dram_tensor("wq", [D, D], F32, kind="ExternalInput").ap()
    wk = nc.dram_tensor("wk", [D, D], F32, kind="ExternalInput").ap()
    wv = nc.dram_tensor("wv", [D, D], F32, kind="ExternalInput").ap()
    wo = nc.dram_tensor("wo", [D, D], F32, kind="ExternalInput").ap()
    bor = nc.dram_tensor("bor", [128, C], F32, kind="ExternalInput").ap()
    qs = nc.dram_tensor("qs", [D, S], BF16, kind="Internal").ap()
    ks = nc.dram_tensor("ks", [D, S], BF16, kind="Internal").ap()
    vs = nc.dram_tensor("vs", [D, S], BF16, kind="Internal").ap()
    ctx = contextlib.ExitStack()
    with ctx:
        p = P(nc, ctx.enter_context)
        cst = build_consts(p)
        sec_proj(p, cst, xin, [qs, ks, vs], [wq, wk, wv], "pj0_")
        sec_attn0(p, cst, xin, xout, qs, ks, vs, wo, bor, "a0_")
        p.wait("sync", "do", p.sc["do"])
        emit(nc, p)
    return nc


def build_attn0_debug():
    nc = bass.Bass("TRN2", target_bir_lowering=False, debug=False)
    xin = nc.dram_tensor("xin", [D, S], F32, kind="ExternalInput").ap()
    xout = nc.dram_tensor("xout", [D, S], F32, kind="ExternalOutput").ap()
    wq = nc.dram_tensor("wq", [D, D], F32, kind="ExternalInput").ap()
    wk = nc.dram_tensor("wk", [D, D], F32, kind="ExternalInput").ap()
    wv = nc.dram_tensor("wv", [D, D], F32, kind="ExternalInput").ap()
    wo = nc.dram_tensor("wo", [D, D], F32, kind="ExternalInput").ap()
    bor = nc.dram_tensor("bor", [128, C], F32, kind="ExternalInput").ap()
    qs = nc.dram_tensor("qs", [D, S], BF16, kind="ExternalOutput").ap()
    ks = nc.dram_tensor("ks", [D, S], BF16, kind="Internal").ap()
    vs = nc.dram_tensor("vs", [D, S], BF16, kind="Internal").ap()
    dbg = nc.dram_tensor("dbg", [128, S], F32, kind="ExternalOutput").ap()
    dbg2 = nc.dram_tensor("dbg2", [128, S], F32, kind="ExternalOutput").ap()
    dbg3 = nc.dram_tensor("dbg3", [128, S + 256], F32, kind="ExternalOutput").ap()
    dbg4 = nc.dram_tensor("dbg4", [128, S], F32, kind="ExternalOutput").ap()
    dbg5 = nc.dram_tensor("dbg5", [128, 18], F32, kind="ExternalOutput").ap()
    ctx = contextlib.ExitStack()
    with ctx:
        p = P(nc, ctx.enter_context)
        cst = build_consts(p)
        sec_proj(p, cst, xin, [qs, ks, vs], [wq, wk, wv], "pj0_")
        sec_attn0(p, cst, xin, xout, qs, ks, vs, wo, bor, "a0_", dbg=(dbg, dbg2, dbg3, dbg4, dbg5), only_hp0=False, skip_flash=False)
        p.wait("sync", "do", p.sc["do"])
        emit(nc, p)
    return nc


def sec_attn1(p, cst, src, dst, qs, ks, vs, wo, tag, extra_out=None, dbg=None):
    scope = []

    def sb(name, shape, dtype):
        t, cm = p.sbuf(tag + name, shape, dtype)
        scope.append(cm)
        return t

    ctxT = sb("ctxT", [128, C * S], BF16)

    scope_b = []

    def sbb(name, shape, dtype):
        t, cm = p.sbuf(tag + name, shape, dtype)
        scope_b.append(cm)
        return t

    def psb(name, shape, dtype=F32):
        t, cm = p.psum(tag + name, shape, dtype)
        scope_b.append(cm)
        return t

    q2 = sbb("q2", [128, S], BF16)
    k2 = sbb("k2", [128, S], BF16)
    v2 = sbb("v2", [128, S], BF16)
    vtok = sbb("vtok", [128, 2 * 2048], BF16)
    eA = [sbb("eA0", [128, S], BF16), sbb("eA1", [128, S], BF16)]
    eB = [sbb("eB0", [128, S], BF16), sbb("eB1", [128, S], BF16)]
    ptA = sbb("ptA", [128, 32 * 512], BF16)
    ptB = sbb("ptB", [128, 32 * 512], BF16)
    sq2 = eA[0]
    rowr = sbb("rowr", [1, 512], F32)
    smf = sbb("smf", [1, 8], F32)
    smf2 = sbb("smf2", [1, 8], F32)
    nbt = sbb("nbt", [128, 2], F32)
    sg = sbb("sg", [128, 64], F32)
    mu = sbb("mu", [128, 64], F32)
    sca = sbb("sca", [128, 64], F32)
    scb = sbb("scb", [128, 64], F32)
    scd = sbb("scd", [128, 64], F32)
    sce = sbb("sce", [128, 64], F32)
    scf = sbb("scf", [128, 64], F32)
    lw = sbb("lw", [128, 64], F32)
    wbf = sbb("wbf", [128, 64], BF16)
    psS = [[psb("psSA0", [128, 512]), psb("psSA1", [128, 512])],
           [psb("psSB0", [128, 512]), psb("psSB1", [128, 512])]]
    pvv = [psb("pvA", [64, 512]), psb("pvB", [64, 512])]
    pmis = psS[0][0]
    ptp = [psb(f"ptp{i}", [128, 1024], BF16) for i in range(2)]

    p.section_barrier()
    p.wait_free("sync", "bootmem")
    for hp in range(C):
        p.wait_free("sync", tag + "qkv")
        for (t_, s_) in ((q2, qs), (k2, ks), (v2, vs)):
            p.dma("sync", t_[:, :], s_[hp * 128:(hp + 1) * 128, :], "dx")
        ld_qkv = p.sc["dx"]

        for head in range(2):
            ide = cst["ident_bf"][head * 64:head * 64 + 64, head * 64:head * 64 + 64]
            for kcg in range(4):
                bank = ptp[kcg % 2]
                p.wait("tensor", "dx", ld_qkv)
                p.wait_free("tensor", tag + f"ptp{kcg % 2}")
                for j in range(8):
                    kc = kcg * 8 + j
                    vsl = v2[head * 64:head * 64 + 64, kc * 128:(kc + 1) * 128]
                    p.op("tensor", lambda e, v=vsl, b=bank, j=j, ide=ide: e.transpose(
                        b[:, j * 64:(j + 1) * 64], v, ide),
                        inc="pe" if j == 7 else None)
                p.wait("vector", "pe")
                p.wait_free("vector", tag + "vtok")
                dstv = vtok[:, head * 2048 + kcg * 512: head * 2048 + (kcg + 1) * 512]
                p.op("vector", lambda e, b=bank, d=dstv: e.tensor_copy(
                    d, b[:, 0:512]), inc="dve")
                p.set_free(tag + f"ptp{kcg % 2}", "dve")

        for which, t_ in ((0, q2), (1, k2)):
            p.wait("scalar", "dx", ld_qkv)
            p.wait("scalar", "pe")
            p.op("scalar", lambda e, t_=t_: e.activation(sq2[:, :], t_[:, :], AF.Square),
                 inc="act")
            sqd = p.sc["act"]
            for head in range(2):
                onescol = cst["onesA_bf"] if head == 0 else cst["onesB_bf"]
                for tt in range(NT):
                    p.wait("tensor", "act", sqd)
                    p.wait_free("tensor", tag + "pmis")
                    p.op("tensor", lambda e, o=onescol, tt=tt: e.matmul(
                        pmis[0:1, :], o[:, :], sq2[:, tt * 512:(tt + 1) * 512],
                        start=True, stop=True), inc="pe")
                    p.wait("vector", "pe")
                    p.op("vector", lambda e, tt=tt: e.reduce_max(
                        smf2[:, tt:tt + 1], pmis[0:1, :], axis=AX.X), inc="dve")
                    p.set_free(tag + "pmis", "dve"); p.set_free(tag + "psS00", "dve")
                p.wait("vector", "dve")
                p.op("vector", lambda e, i=which * 2 + head: e.reduce_max(
                    smf[:, i:i + 1], smf2[:, :], axis=AX.X), inc="dve")
        for head in range(2):
            p.wait("vector", "dve")
            p.op("vector", lambda e, h=head: e.tensor_tensor(
                smf[:, 4 + h:5 + h], smf[:, h:h + 1], smf[:, 2 + h:3 + h], ALU.mult),
                inc="dve")
            p.wait("scalar", "dve")
            p.op("scalar", lambda e, h=head: e.activation(
                smf[:, 6 + h:7 + h], smf[:, 4 + h:5 + h], AF.Sqrt), inc="act")
            p.wait("vector", "act")
            p.op("vector", lambda e, h=head: e.tensor_scalar(
                smf[:, 4 + h:5 + h], smf[:, 6 + h:7 + h], -0.125, None, ALU.mult),
                inc="dve")
            p.wait("tensor", "dve")
            p.wait_free("tensor", tag + "pmis")
            p.op("tensor", lambda e, h=head: e.matmul(
                pmis[:, 0:1], cst["ones_row_f"][:, :], smf[:, 4 + h:5 + h],
                start=True, stop=True), inc="pe")
            p.wait("scalar", "pe")
            p.op("scalar", lambda e, h=head: e.activation(
                nbt[:, h:h + 1], pmis[:, 0:1], AF.Copy), inc="act")
            p.set_free(tag + "pmis", "act"); p.set_free(tag + "psS00", "act")
        nb_done = p.sc["act"]

        def s_exp(qt):
            qb = qt % 2
            for kt in range(NT):
                kb = kt % 2
                w = max(nb_done,
                        p.free_at.get(tag + f"psS0{kb}", ("act", 0))[1],
                        p.free_at.get(tag + f"psS1{kb}", ("act", 0))[1])
                p.wait("tensor", "act", w)
                p.wait_free("tensor", tag + "pmis")
                if kt == 0:
                    p.wait_free("tensor", tag + f"e0{qb}")
                    p.wait_free("tensor", tag + f"e1{qb}")
                for head, ps_ in ((0, psS[0][kb]), (1, psS[1][kb])):
                    lhs = q2[head * 64:head * 64 + 64, qt * 128:(qt + 1) * 128]
                    rhs = k2[head * 64:head * 64 + 64, kt * 512:(kt + 1) * 512]
                    p.op("tensor", lambda e, l=lhs, r=rhs, b=ps_: e.matmul(
                        b[:, :], l, r, start=True, stop=True),
                        inc="pe" if head == 1 else None)
                p.wait("scalar", "pe")
                for head, e_, ps_ in ((0, eA[qb], psS[0][kb]), (1, eB[qb], psS[1][kb])):
                    p.op("scalar", lambda e, b=ps_, e_=e_, kt=kt, h=head: e.activation(
                        e_[:, kt * 512:(kt + 1) * 512], b[:, :], AF.Exp,
                        bias=nbt[:, h:h + 1], scale=0.125), inc="act")
                p.set_free(tag + f"psS0{kb}", "act")
                p.set_free(tag + f"psS1{kb}", "act")
                if kb == 0:
                    p.set_free(tag + "pmis", "act")

        def stats_part(qt, exp_done):
            qb = qt % 2
            qtm = qt % 4
            for head, e_, pt_ in ((0, eA[qb], ptA), (1, eB[qb], ptB)):
                e3 = e_[:, :].rearrange("p (b w) -> p b w", w=BKV)
                p.wait("vector", "act", exp_done)
                p.op("vector", lambda e, e3=e3: e.tensor_reduce(
                    sg[:, :], e3, AX.X, ALU.add), inc="dve")
                p.op("vector", lambda e, e3=e3: e.tensor_reduce(
                    mu[:, :], e3, AX.X, ALU.max), inc="dve")
                p.wait("vector", "dve")
                p.op("vector", lambda e: e.tensor_tensor_scan(
                    sca[:, :], mu[:, :], mu[:, :], 0.0, ALU.max, ALU.bypass),
                    inc="dve")
                p.op("vector", lambda e: e.tensor_tensor_scan(
                    scb[:, :], sg[:, :], sg[:, :], 0.0, ALU.add, ALU.bypass),
                    inc="dve")
                p.wait("scalar", "dve")
                p.op("scalar", lambda e: e.activation(sce[:, :], sca[:, :], AF.Ln),
                     inc="act")
                p.op("scalar", lambda e: e.activation(scf[:, :], scb[:, :], AF.Ln),
                     inc="act")
                p.wait("vector", "act")
                p.op("vector", lambda e: e.tensor_tensor_scan(
                    sca[:, :], sce[:, :], sce[:, :], 0.0, ALU.add, ALU.bypass),
                    inc="dve")
                p.op("vector", lambda e: e.tensor_tensor_scan(
                    scb[:, :], scf[:, :], scf[:, :], 0.0, ALU.add, ALU.bypass),
                    inc="dve")
                p.wait("vector", "dve")
                p.op("vector", lambda e: e.tensor_tensor(
                    scd[:, :], scb[:, :], sca[:, :], ALU.subtract), inc="dve")
                p.op("vector", lambda e: e.tensor_tensor(
                    lw[:, 0:1], sca[:, 62:63], scb[:, 63:64], ALU.subtract), inc="dve")
                p.wait("vector", "dve")
                p.op("vector", lambda e: e.tensor_scalar(
                    lw[:, 1:64], scd[:, 0:63], lw[:, 0:1], None, ALU.add), inc="dve")
                p.wait("scalar", "dve")
                p.op("scalar", lambda e: e.activation(wbf[:, :], lw[:, :], AF.Exp),
                     inc="act")
                p.wait("vector", "act")
                sd = []
                for sg_ in range(4):
                    wb3 = wbf[:, sg_ * 16:(sg_ + 1) * 16].unsqueeze(2).broadcast_to(
                        [128, 16, BKV])
                    e3s = e_[:, sg_ * 1024:(sg_ + 1) * 1024].rearrange(
                        "p (b w) -> p b w", w=BKV)
                    p.op("vector", lambda e, e3s=e3s, wb3=wb3: e.tensor_tensor(
                        e3s, e3s, wb3, ALU.mult), inc="dve")
                    sd.append(p.sc["dve"])
                scale_dones[(qt, head)] = sd

        def tr_part(qt):
            qb = qt % 2
            qtm = qt % 4
            for head, e_, pt_ in ((0, eA[qb], ptA), (1, eB[qb], ptB)):
                sd = scale_dones.pop((qt, head))
                for kcg in range(4):
                    bank = ptp[kcg % 2]
                    p.wait("tensor", "dve", sd[kcg])
                    p.wait_free("tensor", tag + f"ptp{kcg % 2}")
                    p.wait_free("tensor", tag + "ptAB")
                    for j in range(8):
                        kc = kcg * 8 + j
                        p.op("tensor", lambda e, p_=e_, b=bank, j=j, kc=kc: e.transpose(
                            b[:, j * 128:(j + 1) * 128],
                            p_[:, kc * 128:(kc + 1) * 128], cst["ident_bf"][:, :]),
                            inc="pe" if j == 7 else None)
                    p.wait("scalar", "pe")
                    dsl = pt_[:, qtm * S + kcg * 1024: qtm * S + (kcg + 1) * 1024]
                    p.op("scalar", lambda e, b=bank, d=dsl: e.activation(
                        d, b[:, :], AF.Copy), inc="act")
                    p.set_free(tag + f"ptp{kcg % 2}", "act")
                p.set_free(tag + f"e{head}{qb}", "pe")

        def av(qg):
            pt_done = p.sc["act"]
            for head, pt_, pv_ in ((0, ptA, pvv[0]), (1, ptB, pvv[1])):
                p.wait("tensor", "act", pt_done)
                p.wait_free("tensor", tag + f"pv{head}")
                pt3 = pt_[:, :].rearrange("p (m n) -> p m n", n=128)
                for kc in range(32):
                    lhs = vtok[:, head * 2048 + kc * 64: head * 2048 + (kc + 1) * 64]
                    rhs = pt3[:, kc::32, :]
                    p.op("tensor", lambda e, l=lhs, r=rhs, b=pv_, s=(kc == 0),
                         t=(kc == 31): e.matmul(b[:, :], l, r, start=s, stop=t),
                         inc="pe" if kc == 31 else None)
            p.set_free(tag + "ptAB", "pe")
            p.wait("scalar", "pe")
            p.op("scalar", lambda e, hp=hp, qg=qg: e.activation(
                ctxT[0:64, hp * S + qg * 512: hp * S + (qg + 1) * 512],
                pvv[0][:, :], AF.Copy), inc="act")
            p.op("scalar", lambda e, hp=hp, qg=qg: e.activation(
                ctxT[64:128, hp * S + qg * 512: hp * S + (qg + 1) * 512],
                pvv[1][:, :], AF.Copy), inc="act")
            p.set_free(tag + "pv0", "act")
            p.set_free(tag + "pv1", "act")

        exp_dones = {}
        scale_dones = {}
        for step in range(33):
            if step >= 1:
                stats_part(step - 1, exp_dones.pop(step - 1))
            if step < 32:
                s_exp(step)
                exp_dones[step] = p.sc["act"]
            if step >= 1:
                tr_part(step - 1)
            if step >= 4 and step % 4 == 0:
                av(step // 4 - 1)
        p.set_free(tag + "qkv", "pe")
        p.set_free(tag + "vtok", "pe")

    if dbg is not None:
        p.wait("gpsimd", "act")
        for hp_ in range(C):
            p.dma("gpsimd", dbg[1][:, hp_ * 512:(hp_ + 1) * 512],
                  ctxT[:, hp_ * S: hp_ * S + 512].rearrange("p n -> p n"), "do")
        for engx in ("vector", "scalar", "tensor", "gpsimd", "sync"):
            p.wait(engx, "do")

    for cm in reversed(scope_b):
        cm.__exit__(None, None, None)

    outproj_ln(p, cst, ctxT, src, dst, wo, None, tag, extra_out=extra_out)

    for cm in reversed(scope):
        cm.__exit__(None, None, None)


def build_attn1_only():
    nc = bass.Bass("TRN2", target_bir_lowering=False, debug=False)
    xin = nc.dram_tensor("xin", [D, S], F32, kind="ExternalInput").ap()
    xout = nc.dram_tensor("xout", [D, S], F32, kind="ExternalOutput").ap()
    wq = nc.dram_tensor("wq", [D, D], F32, kind="ExternalInput").ap()
    wk = nc.dram_tensor("wk", [D, D], F32, kind="ExternalInput").ap()
    wv = nc.dram_tensor("wv", [D, D], F32, kind="ExternalInput").ap()
    wo = nc.dram_tensor("wo", [D, D], F32, kind="ExternalInput").ap()
    qs = nc.dram_tensor("qs", [D, S], BF16, kind="Internal").ap()
    ks = nc.dram_tensor("ks", [D, S], BF16, kind="Internal").ap()
    vs = nc.dram_tensor("vs", [D, S], BF16, kind="Internal").ap()
    ctx = contextlib.ExitStack()
    with ctx:
        p = P(nc, ctx.enter_context)
        cst = build_consts(p)
        sec_proj(p, cst, xin, [qs, ks, vs], [wq, wk, wv], "pj1_")
        sec_attn1(p, cst, xin, xout, qs, ks, vs, wo, "a1_")
        p.wait("sync", "do", p.sc["do"])
        emit(nc, p)
    return nc


def build_attn1_debug():
    nc = bass.Bass("TRN2", target_bir_lowering=False, debug=False)
    xin = nc.dram_tensor("xin", [D, S], F32, kind="ExternalInput").ap()
    xout = nc.dram_tensor("xout", [D, S], F32, kind="ExternalOutput").ap()
    wq = nc.dram_tensor("wq", [D, D], F32, kind="ExternalInput").ap()
    wk = nc.dram_tensor("wk", [D, D], F32, kind="ExternalInput").ap()
    wv = nc.dram_tensor("wv", [D, D], F32, kind="ExternalInput").ap()
    wo = nc.dram_tensor("wo", [D, D], F32, kind="ExternalInput").ap()
    qs = nc.dram_tensor("qs", [D, S], BF16, kind="Internal").ap()
    ks = nc.dram_tensor("ks", [D, S], BF16, kind="Internal").ap()
    vs = nc.dram_tensor("vs", [D, S], BF16, kind="Internal").ap()
    dbg = nc.dram_tensor("dbg", [128, S], F32, kind="ExternalOutput").ap()
    dbg2 = nc.dram_tensor("dbg2", [128, S], F32, kind="ExternalOutput").ap()
    dbg3 = nc.dram_tensor("dbg3", [128, 256], F32, kind="ExternalOutput").ap()
    dbg4 = nc.dram_tensor("dbg4", [128, S], F32, kind="ExternalOutput").ap()
    dbg6 = nc.dram_tensor("dbg6", [128, 256], F32, kind="ExternalOutput").ap()
    ctx = contextlib.ExitStack()
    with ctx:
        p = P(nc, ctx.enter_context)
        cst = build_consts(p)
        sec_proj(p, cst, xin, [qs, ks, vs], [wq, wk, wv], "pj1_")
        sec_attn1(p, cst, xin, xout, qs, ks, vs, wo, "a1_", dbg=(dbg, dbg2, dbg3, dbg4, dbg6))
        p.wait("sync", "do", p.sc["do"])
        emit(nc, p)
    return nc


def sec_moe(p, cst, src, dst, wgT, W1, b1all, W2, b2all, acc, tag, extra_out=None, dbg=None):
    nc = p.nc
    cws = nc.dram_tensor(tag + "cws", [8, S], BF16, kind="Internal").ap()
    scope = []

    def sb(name, shape, dtype):
        t, cm = p.sbuf(tag + name, shape, dtype)
        scope.append(cm)
        return t

    gsco = []

    def sbg(name, shape, dtype):
        t, cm = p.sbuf(tag + name, shape, dtype)
        gsco.append(cm)
        return t

    def psg(name, shape, dtype=F32):
        t, cm = p.psum(tag + name, shape, dtype)
        gsco.append(cm)
        return t

    p.section_barrier()
    cwrow = sbg("cwrow", [8, S], BF16)
    wgb = sbg("wgb", [128, C * 8], BF16)
    wgs = sbg("wgs", [128, C * 8], F32)
    xfg = sbg("xfg", [128, C * 512], F32)
    xbg = sbg("xbg", [128, C * 512], BF16)
    gts = sbg("gts", [8, 512], F32)
    gtok = sbg("gtok", [128, 8], F32)
    gt1 = sbg("gt1", [128, 8], F32)
    gt2 = sbg("gt2", [128, 8], F32)
    gtb = sbg("gtb", [128, 8], BF16)
    gm1 = sbg("gm1", [128, 1], F32)
    gm2 = sbg("gm2", [128, 1], F32)
    gm3 = sbg("gm3", [128, 1], F32)
    gm4 = sbg("gm4", [128, 1], F32)
    gt3 = sbg("gt3", [128, 8], F32)
    pg = psg("pg", [8, 512])
    pmg = psg("pmg", [128, 512])
    pmgb = psg("pmgb", [128, 1024], BF16)

    p.dma("sync", wgs[:, :], wgT[:, :], "dw")
    p.wait("vector", "dw")
    p.op("vector", lambda e: e.tensor_copy(wgb[:, :], wgs[:, :]), inc="dve")
    wg_done = p.sc["dve"]
    for tt in range(NT):
        p.wait_free("sync", tag + "xfg")
        p.dma("sync", xfg[:, :].rearrange("p (c n) -> p c n", n=512), x_hbm_tile(src, tt), "dx")
        p.wait("vector", "dx")
        p.op("vector", lambda e: e.tensor_copy(xbg[:, :], xfg[:, :]), inc="dve")
        p.set_free(tag + "xfg", "dve")
        gcast = p.sc["dve"]
        for c in range(4):
            p.wait("tensor", "dve", gcast)
            p.wait_free("tensor", tag + "pmg")
            for fc in range(C):
                lhs = xbg[:, fc * 512 + c * 128: fc * 512 + (c + 1) * 128]
                p.op("tensor", lambda e, l=lhs, fc=fc, s=(fc == 0), t=(fc == C - 1):
                     e.matmul(pmg[0:128, 0:8], l, wgb[:, fc * 8:(fc + 1) * 8],
                              start=s, stop=t), inc="pe" if fc == C - 1 else None)
            p.wait("vector", "pe")
            p.op("vector", lambda e: e.tensor_copy(gtok[:, :], pmg[0:128, 0:8]), inc="dve")
            p.set_free(tag + "pmg", "dve")
            p.wait("vector", "dve")
            p.op("vector", lambda e: e.reduce_max(gm1[:, :], gtok[:, :], axis=AX.X,
                                                  negate=True), inc="dve")
            p.wait("scalar", "dve")
            p.op("scalar", lambda e: e.activation(gt1[:, :], gtok[:, :], AF.Exp,
                                                  bias=gm1[:, :], scale=1.0), inc="act")
            p.wait("vector", "act")
            p.op("vector", lambda e: e.reduce_sum(gm2[:, :], gt1[:, :], axis=AX.X),
                 inc="dve")
            p.wait("vector", "dve")
            p.op("vector", lambda e: e.reciprocal(gm3[:, :], gm2[:, :]), inc="dve")
            p.op("vector", lambda e: e.tensor_scalar(gm4[:, :], gm1[:, :], -1.0,
                                                     None, ALU.mult), inc="dve")
            p.wait("vector", "dve")
            p.op("vector", lambda e: e.tensor_scalar(gt3[:, :], gt1[:, :], gm3[:, :],
                                                     None, ALU.mult), inc="dve")
            p.op("vector", lambda e: e.tensor_scalar(gt2[:, :], gtok[:, :], gm4[:, :],
                                                     None, ALU.is_ge), inc="dve")
            p.wait("vector", "dve")
            if dbg is not None and tt == 7 and c == 3:
                p.wait("gpsimd", "dve")
                p.dma("gpsimd", dbg[1][:, 2000:2008], gt2[:, :], "do")
                p.dma("gpsimd", dbg[1][:, 2008:2016], gt1[:, :], "do")
                for engx in ("vector", "scalar", "tensor", "gpsimd", "sync"):
                    p.wait(engx, "do")
            p.op("vector", lambda e: e.tensor_scalar(gt2[:, :], gt2[:, :], 1e30,
                                                     None, ALU.mult), inc="dve")
            p.wait("vector", "dve")
            p.op("vector", lambda e: e.tensor_tensor(gt2[:, :], gtok[:, :], gt2[:, :],
                                                     ALU.subtract), inc="dve")
            p.wait("vector", "dve")
            p.op("vector", lambda e: e.reduce_max(gm2[:, :], gt2[:, :], axis=AX.X),
                 inc="dve")
            p.wait("vector", "dve")
            p.op("vector", lambda e: e.tensor_scalar(gt2[:, :], gtok[:, :], gm2[:, :],
                                                     None, ALU.is_ge), inc="dve")
            p.wait("vector", "dve")
            p.op("vector", lambda e: e.tensor_tensor(gt1[:, :], gt3[:, :], gt2[:, :],
                                                     ALU.mult), inc="dve")
            p.wait("vector", "dve")
            if dbg is not None and tt == 7 and c == 3:
                p.wait("gpsimd", "dve")
                p.dma("gpsimd", dbg[1][:, 3100:3164], wgs[:, :], "do")
                p.dma("gpsimd", dbg[1][:, 3200:3712], xbg[:, 0:512], "do")
                p.dma("gpsimd", dbg[1][:, 3000:3008], gtok[:, :], "do")
                p.dma("gpsimd", dbg[1][:, 3008:3016], gt1[:, :], "do")
                p.dma("gpsimd", dbg[1][:, 3016:3024], gt2[:, :], "do")
                for engx in ("vector", "scalar", "tensor", "gpsimd", "sync"):
                    p.wait(engx, "do")
            p.op("vector", lambda e: e.tensor_copy(gtb[:, :], gt1[:, :]), inc="dve")
            p.wait("tensor", "dve")
            p.wait_free("tensor", tag + "pmgb")
            p.op("tensor", lambda e: e.transpose(
                pmgb[0:8, 0:128], gtb[:, 0:8], cst["ident_bf"][:, :]), inc="pe")
            p.wait("scalar", "pe")
            p.op("scalar", lambda e, tt=tt, c=c: e.activation(
                cwrow[:, tt * 512 + c * 128: tt * 512 + (c + 1) * 128],
                pmgb[0:8, 0:128], AF.Copy), inc="act")
            p.set_free(tag + "pmgb", "act")
    if dbg is not None:
        p.wait("gpsimd", "act")
        p.dma("gpsimd", dbg[0][0:8, 0:S], cwrow[:, :].rearrange("p n -> p n"), "do")
        for engx in ("vector", "scalar", "tensor", "gpsimd", "sync"):
            p.wait(engx, "do")
    p.wait("gpsimd", "act")
    p.dma("gpsimd", cws[:, :], cwrow[:, :].rearrange("p n -> p n"), "do")
    p.wait("gpsimd", "act")
    p.dma("gpsimd", cws[:, :], cwrow[:, :].rearrange("p n -> p n"), "do")
    for cm in reversed(gsco):
        cm.__exit__(None, None, None)
    cw_done = p.sc["act"]

    esco = []

    def sbe(name, shape, dtype):
        t, cm = p.sbuf(tag + name, shape, dtype)
        esco.append(cm)
        return t

    def pse(name, shape, dtype=F32):
        t, cm = p.psum(tag + name, shape, dtype)
        esco.append(cm)
        return t

    p.section_barrier()
    w1b = sbe("w1b", [128, C * FF], BF16)
    w2b = sbe("w2b", [128, (FF // 128) * D], BF16)
    b1t = sbe("b1t", [128, FF // 128], F32)
    b2t = sbe("b2t", [128, C], F32)
    cwt = sbe("cwt", [8, 1024], BF16)
    xb = sbe("xb", [128, 2 * C * 512], BF16)
    hb = sbe("hb", [128, 2 * 8 * 512], BF16)
    yac = sbe("yac", [128, 2 * C * 512], F32)
    ph = [pse(f"ph{i}", [128, 512]) for i in range(4)]
    py = [pse(f"py{i}", [128, 512]) for i in range(4)]
    pcw = ph[0]

    for ex in range(NE):
        p.dma("sync", b1t[:, :], b1all[:, ex * 32:(ex + 1) * 32], "dw")
        p.dma("sync", b2t[:, :], b2all[:, ex * 8:(ex + 1) * 8], "dw")
        p.wait_free("gpsimd", tag + "w1b")
        for fc in range(C):
            p.dma("gpsimd", w1b[:, fc * FF:(fc + 1) * FF],
                  W1[ex][fc * 128:(fc + 1) * 128, :], "dw")
        p.wait_free("gpsimd", tag + "w2b")
        for g in range(8):
            p.dma("gpsimd", w2b[:, g * 4 * D:(g + 1) * 4 * D].rearrange(
                "p (c n) -> p c n", n=D),
                W2[ex][g * 512:(g + 1) * 512, :].rearrange("(c p) n -> p c n", p=128), "dw")
        wld = p.sc["dw"]

        for tp_ in range(NT // 2):
            t0, t1 = 2 * tp_, 2 * tp_ + 1
            p.wait_free("sync", tag + "cwt")
            p.dma("sync", cwt[:, :], cws[:, t0 * 512:(t0 + 2) * 512], "dx")
            p.wait_free("sync", tag + "xb")
            p.dma("gpsimd", xb[:, 0:C * 512].rearrange("p (c n) -> p c n", n=512),
                  x_hbm_tile(src, t0), "dx")
            p.dma("gpsimd", xb[:, C * 512:2 * C * 512].rearrange("p (c n) -> p c n", n=512),
                  x_hbm_tile(src, t1), "dx")
            xld = p.sc["dx"]
            for g in range(4):
                for hc in range(8):
                    ghc = g * 8 + hc
                    p.wait("tensor", "dx", xld)
                    p.wait("tensor", "dw", wld)
                    p.wait_free("tensor", tag + f"ph0{hc % 2}")
                    p.wait_free("tensor", tag + f"ph1{hc % 2}")
                    if hc % 2 == 0:
                        p.wait_free("tensor", tag + "pcw")
                    for fc in range(C):
                        lhs = w1b[:, fc * FF + ghc * 128: fc * FF + (ghc + 1) * 128]
                        for ti, bank in ((0, ph[hc % 2]), (1, ph[2 + hc % 2])):
                            rhs = xb[:, ti * C * 512 + fc * 512: ti * C * 512 + (fc + 1) * 512]
                            p.op("tensor", lambda e, l=lhs, r=rhs, b=bank, s=(fc == 0),
                                 t=(fc == C - 1): e.matmul(b[:, :], l, r, start=s, stop=t),
                                 inc="pe" if (fc == C - 1 and ti == 1) else None)
                    p.wait("scalar", "pe")
                    for ti, bank in ((0, ph[hc % 2]), (1, ph[2 + hc % 2])):
                        p.op("scalar", lambda e, ghc=ghc, hc=hc, b=bank, ti=ti: e.activation(
                            hb[:, ti * 8 * 512 + hc * 512: ti * 8 * 512 + (hc + 1) * 512],
                            b[:, :], AF.Gelu, bias=b1t[:, ghc:ghc + 1], scale=1.0),
                            inc="act")
                        p.set_free(tag + f"ph{ti}{hc % 2}", "act")
                gelu_done = p.sc["act"]
                for ec in range(C):
                    p.wait("tensor", "act", gelu_done)
                    p.wait_free("tensor", tag + f"py0{ec % 2}")
                    p.wait_free("tensor", tag + f"py1{ec % 2}")
                    for hc in range(8):
                        ghc = g * 8 + hc
                        lhs = w2b[:, ghc * D + ec * 128: ghc * D + (ec + 1) * 128]
                        for ti, bank in ((0, py[ec % 2]), (1, py[2 + ec % 2])):
                            rhs = hb[:, ti * 8 * 512 + hc * 512: ti * 8 * 512 + (hc + 1) * 512]
                            p.op("tensor", lambda e, l=lhs, r=rhs, b=bank, s=(hc == 0),
                                 t=(hc == 7): e.matmul(b[:, :], l, r, start=s, stop=t),
                                 inc="pe" if (hc == 7 and ti == 1) else None)
                    for ti in (0, 1):
                        bank = py[2 * ti + ec % 2] if False else (py[ec % 2] if ti == 0 else py[2 + ec % 2])
                        yslice = yac[:, ti * C * 512 + ec * 512: ti * C * 512 + (ec + 1) * 512]
                        if g == 0:
                            p.wait("scalar", "pe")
                            p.wait("scalar", "do")
                            p.op("scalar", lambda e, y=yslice, b=bank, ec=ec: e.activation(
                                y, b[:, :], AF.Identity, bias=b2t[:, ec:ec + 1],
                                scale=1.0), inc="act")
                            p.set_free(tag + f"py{ti}{ec % 2}", "act")
                        else:
                            p.wait("vector", "pe")
                            p.op("vector", lambda e, y=yslice, b=bank: e.tensor_tensor(
                                y, y, b[:, :], ALU.add), inc="dve")
                            p.set_free(tag + f"py{ti}{ec % 2}", "dve")
            for ti, tt in ((0, t0), (1, t1)):
                p.wait("tensor", "dx", xld)
                p.wait_free("tensor", tag + "pcw")
                p.op("tensor", lambda e, ex=ex, ti=ti: e.matmul(
                    pcw[:, :], cst["sel8"][0:8, ex * 128:(ex + 1) * 128],
                    cwt[0:8, ti * 512:(ti + 1) * 512], start=True, stop=True), inc="pe")
                p.wait("vector", "pe")
                p.wait("vector", "act")
                for oc in range(C):
                    p.op("vector", lambda e, oc=oc, ti=ti: e.tensor_tensor(
                        yac[:, ti * C * 512 + oc * 512: ti * C * 512 + (oc + 1) * 512],
                        yac[:, ti * C * 512 + oc * 512: ti * C * 512 + (oc + 1) * 512],
                        pcw[:, :], ALU.mult), inc="dve")
                p.set_free(tag + "pcw", "dve"); p.set_free(tag + "ph00", "dve")
                p.wait("gpsimd", "dve")
                p.wait_free("gpsimd", tag + f"acc{tt}")
                ysrc = yac[:, ti * C * 512:(ti + 1) * C * 512].rearrange(
                    "p (c n) -> p c n", n=512)
                if ex == 0:
                    p.dma("gpsimd", x_hbm_tile(acc, tt), ysrc, "do")
                else:
                    h, _ = p.inc("do", 16)
                    ot = x_hbm_tile(acc, tt)
                    p.R.op("gpsimd", lambda e, o=ot, h=h, ysrc=ysrc: e.dma_start(
                        out=o, in_=ysrc, accum_op=ALU.add).then_inc(h, 16))
                p.set_free(tag + f"acc{tt}", "do")
            p.set_free(tag + "xb", "do")
            p.set_free(tag + "cwt", "pe")
        p.set_free(tag + "w1b", "pe")
        p.set_free(tag + "w2b", "pe")
    for cm in reversed(esco):
        cm.__exit__(None, None, None)

    fsco = []

    def sbf(name, shape, dtype):
        t, cm = p.sbuf(tag + name, shape, dtype)
        fsco.append(cm)
        return t

    def psf(name, shape, dtype=F32):
        t, cm = p.psum(tag + name, shape, dtype)
        fsco.append(cm)
        return t

    p.section_barrier()
    xf2 = [sbf(f"fxf{i}", [128, C * 512], F32) for i in range(2)]
    ya2 = sbf("fya", [128, C * 512], F32)
    zb2 = sbf("fzb", [128, C * 512], BF16)
    sq2 = sbf("fsq", [128, C * 512], BF16)
    rowa = sbf("frowa", [1, 512], F32)
    rowb = sbf("frowb", [1, 512], F32)
    pls = psf("fpls", [1, 512])
    plb = [psf(f"fplb{i}", [128, 512]) for i in range(2)]
    for tt in range(NT):
        cur = xf2[tt % 2]
        p.wait_free("sync", tag + f"fxf{tt % 2}")
        p.dma("sync", cur[:, :].rearrange("p (c n) -> p c n", n=512), x_hbm_tile(src, tt), "dx")
        p.wait_free("sync", tag + "fya")
        p.dma("sync", ya2[:, :].rearrange("p (c n) -> p c n", n=512), x_hbm_tile(acc, tt), "dx")
        p.wait("vector", "dx")
        for oc in range(C):
            p.op("vector", lambda e, oc=oc, cur=cur: e.tensor_tensor(
                ya2[:, oc * 512:(oc + 1) * 512], ya2[:, oc * 512:(oc + 1) * 512],
                cur[:, oc * 512:(oc + 1) * 512], ALU.add), inc="dve")
        if dbg is not None and len(dbg) > 3 and tt == 0:
            p.wait("gpsimd", "dve")
            p.dma("gpsimd", dbg[3][:, :], ya2[:, :].rearrange("p n -> p n"), "do")
            for engx in ("vector", "scalar", "tensor", "gpsimd", "sync"):
                p.wait(engx, "do")
        ln_epilogue(p, cst, tag + f"fxf{tt % 2}|" + tag + "f", tt, ya2, zb2, sq2, cur,
                    rowa, rowb, pls, plb, dst, extra_out=extra_out,
                    dbgrow=(dbg[2],) if (dbg is not None and len(dbg) > 2 and tt == 0) else None)
        p.set_free(tag + "fya", "dve")
    for cm in reversed(fsco):
        cm.__exit__(None, None, None)
    for cm in reversed(scope):
        cm.__exit__(None, None, None)


def build_moe_only():
    nc = bass.Bass("TRN2", target_bir_lowering=False, debug=False)
    xin = nc.dram_tensor("xin", [D, S], F32, kind="ExternalInput").ap()
    xout = nc.dram_tensor("xout", [D, S], F32, kind="ExternalOutput").ap()
    wgT = nc.dram_tensor("wgT", [128, 64], F32, kind="ExternalInput").ap()
    W1 = [nc.dram_tensor(f"W1_{e}", [D, FF], F32, kind="ExternalInput").ap() for e in range(NE)]
    W2 = [nc.dram_tensor(f"W2_{e}", [FF, D], F32, kind="ExternalInput").ap() for e in range(NE)]
    b1all = nc.dram_tensor("b1all", [128, 32 * NE], F32, kind="ExternalInput").ap()
    b2all = nc.dram_tensor("b2all", [128, 8 * NE], F32, kind="ExternalInput").ap()
    acc = nc.dram_tensor("macc", [D, S], F32, kind="Internal").ap()
    ctx = contextlib.ExitStack()
    with ctx:
        p = P(nc, ctx.enter_context)
        cst = build_consts(p)
        sec_moe(p, cst, xin, xout, wgT, W1, b1all, W2, b2all, acc, "m1_")
        p.wait("sync", "do", p.sc["do"])
        emit(nc, p)
    return nc


def build_moe_debug():
    nc = bass.Bass("TRN2", target_bir_lowering=False, debug=False)
    xin = nc.dram_tensor("xin", [D, S], F32, kind="ExternalInput").ap()
    xout = nc.dram_tensor("xout", [D, S], F32, kind="ExternalOutput").ap()
    wgT = nc.dram_tensor("wgT", [128, 64], F32, kind="ExternalInput").ap()
    W1 = [nc.dram_tensor(f"W1_{e}", [D, FF], F32, kind="ExternalInput").ap() for e in range(NE)]
    W2 = [nc.dram_tensor(f"W2_{e}", [FF, D], F32, kind="ExternalInput").ap() for e in range(NE)]
    b1all = nc.dram_tensor("b1all", [128, 32 * NE], F32, kind="ExternalInput").ap()
    b2all = nc.dram_tensor("b2all", [128, 8 * NE], F32, kind="ExternalInput").ap()
    acc = nc.dram_tensor("macc", [D, S], F32, kind="Internal").ap()
    dbg = nc.dram_tensor("dbg", [128, S], F32, kind="ExternalOutput").ap()
    dbg2 = nc.dram_tensor("dbg2", [128, S], F32, kind="ExternalOutput").ap()
    dbg3 = nc.dram_tensor("dbg3", [128, S], F32, kind="ExternalOutput").ap()
    dbg4 = nc.dram_tensor("dbg4", [128, S], F32, kind="ExternalOutput").ap()
    ctx = contextlib.ExitStack()
    with ctx:
        p = P(nc, ctx.enter_context)
        cst = build_consts(p)
        sec_moe(p, cst, xin, xout, wgT, W1, b1all, W2, b2all, acc, "m1_", dbg=(dbg, dbg2, dbg3, dbg4))
        p.wait("sync", "do", p.sc["do"])
        emit(nc, p)
    return nc


# version 12
# speedup vs baseline: 1.0339x; 1.0339x over previous
import contextlib
import numpy as np
import ml_dtypes
import concourse.bass as bass
import concourse.mybir as mybir
from concourse.bass_utils import run_bass_kernel_spmd

F32 = mybir.dt.float32
BF16 = mybir.dt.bfloat16
AF = mybir.ActivationFunctionType
ALU = mybir.AluOpType
AX = mybir.AxisListType

D = 1024
S = 4096
C = D // 128
NT = S // 512
FF = 4096
H = 16
HD = 64
NE = 8
BKV = 64
NBLK = S // BKV

BF = ml_dtypes.bfloat16


class Rec:
    def __init__(self):
        self.items = []

    def op(self, eng, fn):
        self.items.append((eng, fn))


class P:
    def __init__(self, nc, ent):
        self.nc = nc
        self.ent = ent
        self.R = Rec()
        names = ["dx", "dw", "do", "pe", "act", "dve", "gp"]
        self.sh = {n: ent(nc.semaphore(n)) for n in names}
        self.sc = {n: 0 for n in names}
        self.free_at = {}

    def sbuf(self, name, shape, dtype):
        cm = self.nc.sbuf_tensor(name, shape, dtype)
        t = cm.__enter__()
        return t, cm

    def psum(self, name, shape, dtype=F32):
        cm = self.nc.psum_tensor(name, shape, dtype)
        t = cm.__enter__()
        return t, cm

    def wait(self, eng, sem, val=None):
        v = self.sc[sem] if val is None else val
        h = self.sh[sem]
        if v > 0:
            self.R.op(eng, lambda e, h=h, v=v: e.wait_ge(h, v))

    def wait_free(self, eng, tag):
        fa = self.free_at.get(tag)
        if fa is not None:
            sem, cnt = fa
            self.wait(eng, sem, cnt)

    def set_free(self, tag, sem):
        self.free_at[tag] = (sem, self.sc[sem])

    def inc(self, sem, k):
        self.sc[sem] += k
        return self.sh[sem], self.sc[sem]

    def dma(self, q, out, in_, sem="dx"):
        h, _ = self.inc(sem, 16)
        self.R.op(q, lambda e, o=out, i=in_, h=h: e.dma_start(out=o, in_=i).then_inc(h, 16))

    def section_barrier(self):
        for q in ("sync", "gpsimd"):
            for sem in ("pe", "dve", "act", "do"):
                self.wait(q, sem)

    def op(self, eng, fn, inc=None, k=1):
        if inc is None:
            self.R.op(eng, fn)
        else:
            h, _ = self.inc(inc, k)
            self.R.op(eng, lambda e, f=fn, h=h, k=k: f(e).then_inc(h, k))


def x_hbm_tile(xap, tt):
    return xap.rearrange("(c p) n -> p c n", p=128)[:, :, tt * 512:(tt + 1) * 512]


def sec_ffn(p, cst, src, dst, w1, b1r, w2, b2r, tag):
    scope = []

    def sb(name, shape, dtype):
        t, cm = p.sbuf(tag + name, shape, dtype)
        scope.append(cm)
        return t

    def ps(name, shape, dtype=F32):
        t, cm = p.psum(tag + name, shape, dtype)
        scope.append(cm)
        return t

    w1b = sb("w1b", [128, C * FF], BF16)
    w2b = sb("w2b", [128, (FF // 128) * D], BF16)
    b1t = sb("b1t", [128, FF // 128], F32)
    b2t = sb("b2t", [128, C], F32)
    xf = [sb(f"xf{i}", [128, C * 512], F32) for i in range(2)]
    wst = xf
    xb = sb("xb", [128, C * 512], BF16)
    hb = sb("hb", [128, 8 * 512], BF16)
    yac = sb("yac", [128, C * 512], F32)
    rowa = sb("rowa", [1, 512], F32)
    rowb = sb("rowb", [1, 512], F32)
    ph = [ps(f"ph{i}", [128, 512]) for i in range(2)]
    py = [ps(f"py{i}", [128, 512]) for i in range(2)]
    pls = ps("pls", [1, 512])
    plb = [ps(f"plb{i}", [128, 512]) for i in range(2)]

    p.section_barrier()
    p.wait_free("sync", "bootmem")
    p.dma("sync", b1t[:, :], b1r[:, :], "dw")
    p.dma("sync", b2t[:, :], b2r[:, :], "dw")
    for fc in range(C):
        p.wait_free("sync", tag + f"xf{fc % 2}")
        p.dma("sync", wst[fc % 2][:, :], w1[fc * 128:(fc + 1) * 128, :], "dw")
        p.wait("vector", "dw")
        p.op("vector", lambda e, fc=fc: e.tensor_copy(
            w1b[:, fc * FF:(fc + 1) * FF], wst[fc % 2][:, :]), inc="dve")
        p.set_free(tag + f"xf{fc % 2}", "dve")
    for g in range(8):
        p.wait_free("sync", tag + f"xf{g % 2}")
        p.dma("sync", wst[g % 2][:, :].rearrange("p (c n) -> p c n", n=D),
              w2[g * 512:(g + 1) * 512, :].rearrange("(c p) n -> p c n", p=128), "dw")
        p.wait("vector", "dw")
        p.op("vector", lambda e, g=g: e.tensor_copy(
            w2b[:, g * 4 * D:(g + 1) * 4 * D], wst[g % 2][:, :]), inc="dve")
        p.set_free(tag + f"xf{g % 2}", "dve")

    for tt in range(NT):
        cur = xf[tt % 2]
        p.wait_free("sync", tag + f"xf{tt % 2}")
        p.dma("sync", cur[:, :].rearrange("p (c n) -> p c n", n=512), x_hbm_tile(src, tt), "dx")
        p.wait("vector", "dx")
        p.op("vector", lambda e, cur=cur: e.tensor_copy(xb[:, :], cur[:, :]), inc="dve")
        cast_done = p.sc["dve"]

        for g in range(4):
            for hc in range(8):
                ghc = g * 8 + hc
                bank = ph[hc % 2]
                p.wait("tensor", "dve", cast_done)
                p.wait_free("tensor", tag + f"ph{hc % 2}")
                for fc in range(C):
                    lhs = w1b[:, fc * FF + ghc * 128: fc * FF + (ghc + 1) * 128]
                    rhs = xb[:, fc * 512:(fc + 1) * 512]
                    p.op("tensor", lambda e, l=lhs, r=rhs, b=bank, s=(fc == 0), t=(fc == C - 1):
                         e.matmul(b[:, :], l, r, start=s, stop=t),
                         inc="pe" if fc == C - 1 else None)
                p.wait("scalar", "pe")
                p.op("scalar", lambda e, ghc=ghc, hc=hc, b=bank: e.activation(
                    hb[:, hc * 512:(hc + 1) * 512], b[:, :], AF.Gelu,
                    bias=b1t[:, ghc:ghc + 1], scale=1.0), inc="act")
                p.set_free(tag + f"ph{hc % 2}", "act")
            gelu_done = p.sc["act"]
            for ec in range(C):
                bank = py[ec % 2]
                p.wait("tensor", "act", gelu_done)
                p.wait_free("tensor", tag + f"py{ec % 2}")
                for hc in range(8):
                    ghc = g * 8 + hc
                    lhs = w2b[:, ghc * D + ec * 128: ghc * D + (ec + 1) * 128]
                    rhs = hb[:, hc * 512:(hc + 1) * 512]
                    p.op("tensor", lambda e, l=lhs, r=rhs, b=bank, s=(hc == 0), t=(hc == 7):
                         e.matmul(b[:, :], l, r, start=s, stop=t),
                         inc="pe" if hc == 7 else None)
                yslice = yac[:, ec * 512:(ec + 1) * 512]
                if g == 0:
                    p.wait("scalar", "pe")
                    p.op("scalar", lambda e, y=yslice, b=bank, ec=ec: e.activation(
                        y, b[:, :], AF.Identity, bias=b2t[:, ec:ec + 1], scale=1.0), inc="act")
                    p.set_free(tag + f"py{ec % 2}", "act")
                else:
                    p.wait("vector", "pe")
                    p.op("vector", lambda e, y=yslice, b=bank: e.tensor_tensor(
                        y, y, b[:, :], ALU.add), inc="dve")
                    p.set_free(tag + f"py{ec % 2}", "dve")
        ydone_act = p.sc["act"]
        p.wait("vector", "act", ydone_act)
        for oc in range(C):
            p.op("vector", lambda e, oc=oc, cur=cur: e.tensor_tensor(
                yac[:, oc * 512:(oc + 1) * 512], yac[:, oc * 512:(oc + 1) * 512],
                cur[:, oc * 512:(oc + 1) * 512], ALU.add), inc="dve")
        ln_epilogue(p, cst, tag + f"xf{tt % 2}|" + tag, tt, yac, xb, hb, cur,
                    rowa, rowb, pls, plb, dst)

    for cm in reversed(scope):
        cm.__exit__(None, None, None)


def ln_epilogue(p, cst, tag, tt, yac, zbuf, sqbuf, xo, rowa, rowb, pls, plb, dst,
                extra_out=None, dbgrow=None):
    if "|" in tag:
        xo_tag, tag = tag.split("|")
    else:
        xo_tag = tag + "xo"
    """LayerNorm over features of yac (fp32 [128, C*512]); writes fp32 tile to
    dst HBM (feature-major). zbuf: bf16 scratch [128, C*512]; sqbuf: bf16
    scratch (>= [128, C*512]). Trashes yac (centers in place)."""
    p.op("vector", lambda e: e.tensor_copy(zbuf[:, 0:C * 512], yac[:, :]), inc="dve")
    zb_done = p.sc["dve"]
    p.wait("tensor", "dve", zb_done)
    p.wait_free("tensor", tag + "pls")
    for oc in range(C):
        p.op("tensor", lambda e, oc=oc, s=(oc == 0), t=(oc == C - 1): e.matmul(
            pls[:, :], cst["ones_col_bf"][:, :], zbuf[:, oc * 512:(oc + 1) * 512],
            start=s, stop=t), inc="pe" if oc == C - 1 else None)
    p.wait("scalar", "pe")
    p.op("scalar", lambda e: e.activation(rowa[:, :], pls[:, :], AF.Identity,
                                          scale=-1.0 / D), inc="act")
    mean_done = p.sc["act"]
    p.wait("tensor", "act", mean_done)
    p.wait_free("tensor", tag + "plb0")
    p.op("tensor", lambda e: e.matmul(plb[0][:, :], cst["ones_row_f"][:, :],
                                      rowa[:, :], start=True, stop=True), inc="pe")
    p.wait("vector", "pe")
    for oc in range(C):
        p.op("vector", lambda e, oc=oc: e.tensor_tensor(
            yac[:, oc * 512:(oc + 1) * 512], yac[:, oc * 512:(oc + 1) * 512],
            plb[0][:, :], ALU.add), inc="dve")
    p.set_free(tag + "plb0", "dve")
    p.wait("scalar", "dve")
    for oc in range(C):
        p.op("scalar", lambda e, oc=oc: e.activation(
            sqbuf[:, oc * 512:(oc + 1) * 512], yac[:, oc * 512:(oc + 1) * 512],
            AF.Square), inc="act")
    p.wait("tensor", "act")
    for oc in range(C):
        p.op("tensor", lambda e, oc=oc, s=(oc == 0), t=(oc == C - 1): e.matmul(
            pls[:, :], cst["ones_col_bf"][:, :], sqbuf[:, oc * 512:(oc + 1) * 512],
            start=s, stop=t), inc="pe" if oc == C - 1 else None)
    p.wait("scalar", "pe")
    p.op("scalar", lambda e: e.activation(rowb[:, :], pls[:, :], AF.Sqrt,
                                          bias=cst["eps1"][:, :], scale=1.0 / D), inc="act")
    p.set_free(tag + "pls", "act")
    p.wait("vector", "act")
    p.op("vector", lambda e: e.reciprocal(rowa[:, :], rowb[:, :]), inc="dve")
    p.wait("tensor", "dve")
    p.wait_free("tensor", tag + "plb1")
    p.op("tensor", lambda e: e.matmul(plb[1][:, :], cst["ones_row_f"][:, :],
                                      rowa[:, :], start=True, stop=True), inc="pe")
    p.wait("vector", "pe")
    for oc in range(C):
        p.op("vector", lambda e, oc=oc: e.tensor_tensor(
            xo[:, 0:C * 512][:, oc * 512:(oc + 1) * 512], yac[:, oc * 512:(oc + 1) * 512],
            plb[1][:, :], ALU.mult), inc="dve")
    p.set_free(tag + "plb1", "dve")
    p.wait("gpsimd", "dve")
    if dbgrow is not None:
        p.dma("gpsimd", dbgrow[0][0:1, 0:512], rowa[:, :], "do")
        p.dma("gpsimd", dbgrow[0][1:2, 0:512], rowb[:, :], "do")
        p.dma("gpsimd", dbgrow[0][2:34, 0:512], zbuf[0:32, 0:512], "do")
        p.dma("gpsimd", dbgrow[0][34:66, 0:512], xo[0:32, 0:512], "do")
        p.dma("gpsimd", dbgrow[0][66:98, 0:512], yac[0:32, 0:512], "do")
        for engx in ("vector", "scalar", "tensor", "gpsimd", "sync"):
            p.wait(engx, "do")
    p.dma("gpsimd", x_hbm_tile(dst, tt), xo[:, 0:C * 512].rearrange("p (c n) -> p c n", n=512), "do")
    if extra_out is not None:
        p.dma("gpsimd", x_hbm_tile(extra_out, tt),
              xo[:, 0:C * 512].rearrange("p (c n) -> p c n", n=512), "do")
    p.set_free(xo_tag, "do")


def build_consts(p):
    nc = p.nc
    cst = {}
    specs = [
        ("ones_col_bf", [128, 1], BF16),
        ("ones_row_f", [1, 128], F32),
        ("ident_bf", [128, 128], BF16),
        ("onesA_bf", [128, 1], BF16),
        ("onesB_bf", [128, 1], BF16),
        ("onesA_f", [1, 128], F32),
        ("onesB_f", [1, 128], F32),
        ("mask_bf", [128, 2048], BF16),
        ("ident_f", [128, 128], F32),
        ("sel8", [8, 1024], BF16),
    ]
    for name, shape, dt in specs:
        a = nc.dram_tensor("c_" + name, shape, F32, kind="ExternalInput").ap()
        t, _ = p.sbuf("cst_" + name, shape, dt)
        p.dma("gpsimd", t[:, :], a[:, :], "dw")
        cst[name] = t
    t3, _ = p.sbuf("cst_eps1", [1, 1], F32)
    p.op("vector", lambda e: e.memset(t3[:, :], 1e-5), inc="dve")
    cst["eps1"] = t3
    return cst


def const_arrays():
    i = np.arange(128)
    maskv = np.zeros((128, 2048), np.float32)
    for v in range(4):
        j = np.arange(512)
        maskv[:, v * 512:(v + 1) * 512] = (j[None, :] <= v * 128 + i[:, None])
    onesA = np.zeros((128, 1), np.float32); onesA[0:64] = 1.0
    onesB = np.zeros((128, 1), np.float32); onesB[64:128] = 1.0
    return {
        "c_ones_col_bf": np.ones((128, 1), np.float32),
        "c_ones_row_f": np.ones((1, 128), np.float32),
        "c_ident_bf": np.eye(128, dtype=np.float32),
        "c_onesA_bf": onesA, "c_onesB_bf": onesB,
        "c_onesA_f": onesA.T.copy(), "c_onesB_f": onesB.T.copy(),
        "c_mask_bf": maskv,
        "c_ident_f": np.eye(128, dtype=np.float32),
        "c_sel8": np.concatenate([np.eye(8, dtype=np.float32)[:, e:e+1].repeat(128, 1) for e in range(8)], axis=1),
    }


def emit(nc, p):
    with nc.Block() as block:
        for engname in ["sync", "tensor", "vector", "scalar", "gpsimd"]:
            items = [fn for en, fn in p.R.items if en == engname]
            if not items:
                continue
            dec = getattr(block, engname)

            def body(eng, _items=items):
                for fn in _items:
                    fn(eng)
            dec(body)


def build_ffn_only():
    nc = bass.Bass("TRN2", target_bir_lowering=False, debug=False)
    xin = nc.dram_tensor("xin", [D, S], F32, kind="ExternalInput").ap()
    xout = nc.dram_tensor("xout", [D, S], F32, kind="ExternalOutput").ap()
    w1 = nc.dram_tensor("w1", [D, FF], F32, kind="ExternalInput").ap()
    b1r = nc.dram_tensor("b1r", [128, FF // 128], F32, kind="ExternalInput").ap()
    w2 = nc.dram_tensor("w2", [FF, D], F32, kind="ExternalInput").ap()
    b2r = nc.dram_tensor("b2r", [128, C], F32, kind="ExternalInput").ap()
    ctx = contextlib.ExitStack()
    with ctx:
        p = P(nc, ctx.enter_context)
        cst = build_consts(p)
        sec_ffn(p, cst, xin, xout, w1, b1r, w2, b2r, "f0_")
        p.wait("sync", "do", p.sc["do"])
        emit(nc, p)
    return nc


_NC_CACHE = {}


def build_full():
    nc = bass.Bass("TRN2", target_bir_lowering=False, debug=False)
    ap = lambda name, shape, dt=F32, kind="ExternalInput": nc.dram_tensor(name, shape, dt, kind=kind).ap()
    xin = ap("xin", [D, S])
    xout = ap("xout", [D, S], F32, "ExternalOutput")
    wq0, wk0, wv0, wo0 = (ap(n, [D, D]) for n in ("wq0", "wk0", "wv0", "wo0"))
    bor0 = ap("bor0", [128, C])
    w1 = ap("w1", [D, FF]); b1r = ap("b1r", [128, 32])
    w2 = ap("w2", [FF, D]); b2r = ap("b2r", [128, C])
    wq1, wk1, wv1, wo1 = (ap(n, [D, D]) for n in ("wq1", "wk1", "wv1", "wo1"))
    wgT = ap("wgT", [128, 64])
    W1 = [ap(f"W1_{e}", [D, FF]) for e in range(NE)]
    W2 = [ap(f"W2_{e}", [FF, D]) for e in range(NE)]
    b1all = ap("b1all", [128, 32 * NE])
    b2all = ap("b2all", [128, 8 * NE])
    qs = ap("qs", [D, S], BF16, "Internal")
    ks = ap("ks", [D, S], BF16, "Internal")
    vs = ap("vs", [D, S], BF16, "Internal")
    xs_a = ap("xs_a", [D, S], F32, "Internal")
    xs_b = ap("xs_b", [D, S], F32, "Internal")
    xs_c = ap("xs_c", [D, S], F32, "Internal")
    macc = ap("macc", [D, S], F32, "Internal")
    ctx = contextlib.ExitStack()
    with ctx:
        p = P(nc, ctx.enter_context)
        cst = build_consts(p)
        sec_proj(p, cst, xin, [qs, ks, vs], [wq0, wk0, wv0], "pj0_")
        sec_attn0(p, cst, xin, xs_a, qs, ks, vs, wo0, bor0, "a0_")
        sec_ffn(p, cst, xs_a, xs_b, w1, b1r, w2, b2r, "f0_")
        sec_proj(p, cst, xs_b, [qs, ks, vs], [wq1, wk1, wv1], "pj1_")
        sec_attn1(p, cst, xs_b, xs_c, qs, ks, vs, wo1, "a1_")
        sec_moe(p, cst, xs_c, xout, wgT, W1, b1all, W2, b2all, macc, "m1_")
        p.wait("sync", "do", p.sc["do"])
        emit(nc, p)
    return nc


def kernel(L0_attn_x, L0_attn_Wq, L0_attn_Wk, L0_attn_Wv, L0_attn_Wo, L0_attn_bo,
           L0_ffn_w1, L0_ffn_b1, L0_ffn_w2, L0_ffn_b2,
           L1_attn_Wq, L1_attn_Wk, L1_attn_Wv, L1_attn_Wo,
           L1_ffn_w_gate, L1_ffn_W1, L1_ffn_b1, L1_ffn_W2, L1_ffn_b2):
    x = np.asarray(L0_attn_x, dtype=np.float32)
    B = x.shape[0]
    if "nc" not in _NC_CACHE:
        _NC_CACHE["nc"] = build_full()
    nc = _NC_CACHE["nc"]
    f = np.float32
    shared = {
        "wq0": np.ascontiguousarray(L0_attn_Wq, f), "wk0": np.ascontiguousarray(L0_attn_Wk, f),
        "wv0": np.ascontiguousarray(L0_attn_Wv, f), "wo0": np.ascontiguousarray(L0_attn_Wo, f),
        "bor0": np.ascontiguousarray(np.asarray(L0_attn_bo, f).reshape(8, 128).T),
        "w1": np.ascontiguousarray(L0_ffn_w1, f),
        "b1r": np.ascontiguousarray(np.asarray(L0_ffn_b1, f).reshape(32, 128).T),
        "w2": np.ascontiguousarray(L0_ffn_w2, f),
        "b2r": np.ascontiguousarray(np.asarray(L0_ffn_b2, f).reshape(8, 128).T),
        "wq1": np.ascontiguousarray(L1_attn_Wq, f), "wk1": np.ascontiguousarray(L1_attn_Wk, f),
        "wv1": np.ascontiguousarray(L1_attn_Wv, f), "wo1": np.ascontiguousarray(L1_attn_Wo, f),
        "wgT": np.ascontiguousarray(
            np.asarray(L1_ffn_w_gate, f).T.reshape(8, 128, 8).transpose(1, 0, 2).reshape(128, 64)),
        **const_arrays(),
    }
    W1a = np.asarray(L1_ffn_W1, f); W2a = np.asarray(L1_ffn_W2, f)
    b1a = np.asarray(L1_ffn_b1, f); b2a = np.asarray(L1_ffn_b2, f)
    for e in range(NE):
        shared[f"W1_{e}"] = np.ascontiguousarray(W1a[e])
        shared[f"W2_{e}"] = np.ascontiguousarray(W2a[e])
    b1all = np.zeros((128, NE * 32), f)
    b2all = np.zeros((128, NE * 8), f)
    for e in range(NE):
        b1all[:, e * 32:(e + 1) * 32] = b1a[e].reshape(32, 128).T
        b2all[:, e * 8:(e + 1) * 8] = b2a[e].reshape(8, 128).T
    shared["b1all"] = b1all
    shared["b2all"] = b2all
    in_maps = []
    for c in range(B):
        m = dict(shared)
        m["xin"] = np.ascontiguousarray(x[c].T)
        in_maps.append(m)
    res = run_bass_kernel_spmd(nc, in_maps, core_ids=list(range(B)))
    out = np.stack([np.asarray(r["xout"]).T for r in res.results], axis=0)
    return out.astype(np.float32)


def sec_proj(p, cst, src, outs, ws, tag):
    scope = []

    def sb(name, shape, dtype):
        t, cm = p.sbuf(tag + name, shape, dtype)
        scope.append(cm)
        return t

    def ps(name, shape, dtype=F32):
        t, cm = p.psum(tag + name, shape, dtype)
        scope.append(cm)
        return t

    xball = sb("xball", [128, C * S], BF16)
    wb = sb("wb", [128, C * D], BF16)
    wstg = [sb(f"wstg{i}", [128, D], F32) for i in range(2)]
    xstg = [sb(f"xstg{i}", [128, S], F32) for i in range(2)]
    obuf = [sb(f"obuf{i}", [128, S], BF16) for i in range(2)]
    pp = [ps(f"pp{i}", [128, 512]) for i in range(2)]

    p.section_barrier()
    p.wait_free("sync", "bootmem")
    for fc in range(C):
        p.wait_free("sync", tag + f"xstg{fc % 2}")
        p.dma("sync", xstg[fc % 2][:, :], src[fc * 128:(fc + 1) * 128, :], "dx")
        p.wait("vector", "dx")
        p.op("vector", lambda e, fc=fc: e.tensor_copy(
            xball[:, fc * S:(fc + 1) * S], xstg[fc % 2][:, :]), inc="dve")
        p.set_free(tag + f"xstg{fc % 2}", "dve")

    for wi, (w_ap, out_ap) in enumerate(zip(ws, outs)):
        p.wait_free("vector", tag + "wb")
        for fc in range(C):
            p.wait_free("sync", tag + f"wstg{fc % 2}")
            p.dma("sync", wstg[fc % 2][:, :], w_ap[fc * 128:(fc + 1) * 128, :], "dw")
            p.wait("vector", "dw")
            p.op("vector", lambda e, fc=fc: e.tensor_copy(
                wb[:, fc * D:(fc + 1) * D], wstg[fc % 2][:, :]), inc="dve")
            p.set_free(tag + f"wstg{fc % 2}", "dve")
        wcast = p.sc["dve"]
        for oc in range(C):
            ob = obuf[oc % 2]
            p.wait_free("scalar", tag + f"obuf{oc % 2}")
            for tt in range(NT):
                bank = pp[tt % 2]
                p.wait("tensor", "dve", wcast)
                p.wait_free("tensor", tag + f"pp{tt % 2}")
                for fc in range(C):
                    lhs = wb[:, fc * D + oc * 128: fc * D + (oc + 1) * 128]
                    rhs = xball[:, fc * S + tt * 512: fc * S + (tt + 1) * 512]
                    p.op("tensor", lambda e, l=lhs, r=rhs, b=bank, s=(fc == 0), t=(fc == C - 1):
                         e.matmul(b[:, :], l, r, start=s, stop=t),
                         inc="pe" if fc == C - 1 else None)
                p.wait("scalar", "pe")
                p.op("scalar", lambda e, b=bank, tt=tt, ob=ob: e.activation(
                    ob[:, tt * 512:(tt + 1) * 512], b[:, :], AF.Copy), inc="act")
                p.set_free(tag + f"pp{tt % 2}", "act")
            p.wait("gpsimd", "act")
            p.dma("gpsimd", out_ap[oc * 128:(oc + 1) * 128, :], ob[:, :], "do")
            p.set_free(tag + f"obuf{oc % 2}", "do")
        p.set_free(tag + "wb", "pe")

    for cm in reversed(scope):
        cm.__exit__(None, None, None)


def sec_attn0(p, cst, src, dst, qs, ks, vs, wo, bor, tag, extra_out=None, dbg=None, only_hp0=False, skip_flash=False):
    scope = []

    def sb(name, shape, dtype):
        t, cm = p.sbuf(tag + name, shape, dtype)
        scope.append(cm)
        return t

    ctxT = sb("ctxT", [128, C * S], BF16)

    scope_b = []

    def sbb(name, shape, dtype):
        t, cm = p.sbuf(tag + name, shape, dtype)
        scope_b.append(cm)
        return t

    def psb(name, shape, dtype=F32):
        t, cm = p.psum(tag + name, shape, dtype)
        scope_b.append(cm)
        return t

    q2 = sbb("q2", [128, S], BF16)
    k2 = sbb("k2", [128, S], BF16)
    v2 = sbb("v2", [128, S], BF16)
    vte = sbb("vte", [128, 2 * 32 * 65], BF16)
    eA = [sbb("eA0", [128, S], BF16), sbb("eA1", [128, S], BF16)]
    eB = [sbb("eB0", [128, S], BF16), sbb("eB1", [128, S], BF16)]
    ptA = sbb("ptA", [128, 32 * 512], BF16)
    ptB = sbb("ptB", [128, 32 * 512], BF16)
    sq2 = eA[0]
    rowr = sbb("rowr", [1, 512], F32)
    rowr2 = sbb("rowr2", [1, 512], F32)
    smf = sbb("smf", [1, 8], F32)
    smf2 = sbb("smf2", [1, 8], F32)
    nbt = sbb("nbt", [128, 2], F32)
    rbt = sbb("rbt", [128, 512], BF16)
    cxt = sbb("cxt", [128, 512], BF16)
    psS = [[psb("psSA0", [128, 512]), psb("psSA1", [128, 512])],
           [psb("psSB0", [128, 512]), psb("psSB1", [128, 512])]]
    pvv = [psb("pvA", [65, 512]), psb("pvB", [65, 512])]
    pmis = psS[0][0]
    ptp = [psb(f"ptp{i}", [128, 1024], BF16) for i in range(2)]

    p.section_barrier()
    p.wait_free("sync", "bootmem")
    for hp in range(C):
        p.wait_free("sync", tag + "qkv")
        for (t_, s_) in ((q2, qs), (k2, ks), (v2, vs)):
            p.dma("sync", t_[:, :], s_[hp * 128:(hp + 1) * 128, :], "dx")
        ld_qkv = p.sc["dx"]

        p.wait("vector", "dx", ld_qkv)
        p.wait_free("vector", tag + "vte")
        p.op("vector", lambda e: e.memset(vte[:, :], 1.0), inc="dve")
        vone = p.sc["dve"]
        for head in range(2):
            for kcg in range(4):
                bank = ptp[kcg % 2]
                p.wait("tensor", "dve", vone)
                p.wait_free("tensor", tag + f"ptp{kcg % 2}")
                ide = cst["ident_bf"][head * 64:head * 64 + 64, head * 64:head * 64 + 64]
                for j in range(8):
                    kc = kcg * 8 + j
                    vsl = v2[head * 64:head * 64 + 64, kc * 128:(kc + 1) * 128]
                    p.op("tensor", lambda e, v=vsl, b=bank, j=j, ide=ide: e.transpose(
                        b[:, j * 64:(j + 1) * 64], v, ide),
                        inc="pe" if j == 7 else None)
                p.wait("vector", "pe")
                dstv = vte[:, head * 2080 + kcg * 8 * 65: head * 2080 + (kcg * 8 + 8) * 65]
                p.op("vector", lambda e, b=bank, d=dstv: e.tensor_copy(
                    d.rearrange("p (c n) -> p c n", n=65)[:, :, 0:64],
                    b[:, 0:512].rearrange("p (c n) -> p c n", n=64)), inc="dve")
                p.set_free(tag + f"ptp{kcg % 2}", "dve")
        vte_done = p.sc["dve"]

        for which, t_ in ((0, q2), (1, k2)):
            p.wait("scalar", "dx", ld_qkv)
            if which == 1:
                p.wait("scalar", "pe")
            p.op("scalar", lambda e, t_=t_: e.activation(sq2[:, :], t_[:, :], AF.Square),
                 inc="act")
            sqd = p.sc["act"]
            for head in range(2):
                onescol = cst["onesA_bf"] if head == 0 else cst["onesB_bf"]
                for tt in range(NT):
                    p.wait("tensor", "act", sqd)
                    p.wait_free("tensor", tag + "pmis")
                    p.op("tensor", lambda e, o=onescol, tt=tt: e.matmul(
                        pmis[0:1, :], o[:, :], sq2[:, tt * 512:(tt + 1) * 512],
                        start=True, stop=True), inc="pe")
                    p.wait("vector", "pe")
                    p.op("vector", lambda e, tt=tt: e.reduce_max(
                        smf2[:, tt:tt + 1], pmis[0:1, :], axis=AX.X), inc="dve")
                    p.set_free(tag + "pmis", "dve"); p.set_free(tag + "psS00", "dve")
                p.wait("vector", "dve")
                p.op("vector", lambda e, i=which * 2 + head: e.reduce_max(
                    smf[:, i:i + 1], smf2[:, :], axis=AX.X), inc="dve")
        for head in range(2):
            p.wait("vector", "dve")
            p.op("vector", lambda e, h=head: e.tensor_tensor(
                smf[:, 4 + h:5 + h], smf[:, h:h + 1], smf[:, 2 + h:3 + h], ALU.mult),
                inc="dve")
            p.wait("scalar", "dve")
            p.op("scalar", lambda e, h=head: e.activation(
                smf[:, 6 + h:7 + h], smf[:, 4 + h:5 + h], AF.Sqrt), inc="act")
            p.wait("vector", "act")
            p.op("vector", lambda e, h=head: e.tensor_scalar(
                smf[:, 4 + h:5 + h], smf[:, 6 + h:7 + h], -0.125, None, ALU.mult),
                inc="dve")
            p.wait("tensor", "dve")
            p.wait_free("tensor", tag + "pmis")
            p.op("tensor", lambda e, h=head: e.matmul(
                pmis[:, 0:1], cst["ones_row_f"][:, :], smf[:, 4 + h:5 + h],
                start=True, stop=True), inc="pe")
            p.wait("scalar", "pe")
            p.op("scalar", lambda e, h=head: e.activation(
                nbt[:, h:h + 1], pmis[:, 0:1], AF.Copy), inc="act")
            p.set_free(tag + "pmis", "act"); p.set_free(tag + "psS00", "act")
        nb_done = p.sc["act"]
        if dbg is not None and len(dbg) > 4 and hp == 0:
            p.wait("gpsimd", "act", nb_done)
            p.wait("gpsimd", "dve")
            p.dma("gpsimd", dbg[4][:, 0:2], nbt[:, :], "do")
            p.dma("gpsimd", dbg[4][0:1, 2:10], smf[:, :], "do")
            p.dma("gpsimd", dbg[4][0:1, 10:18], smf2[:, :], "do")
            p.dma("gpsimd", dbg[3][:, :], sq2[:, :].rearrange("p n -> p n"), "do")
        if skip_flash:
            p.set_free(tag + "qkv", "pe")
            p.set_free(tag + "vte", "pe")
            break

        def s_exp0(qt):
            qg = qt // 4
            qb = qt % 2
            for kt in range(qg + 1):
                w = max(nb_done,
                        p.free_at.get(tag + f"psS0{kt % 2}", ("act", 0))[1],
                        p.free_at.get(tag + f"psS1{kt % 2}", ("act", 0))[1])
                p.wait("tensor", "act", w)
                p.wait_free("tensor", tag + "pmis")
                if kt == 0:
                    p.wait_free("tensor", tag + f"e0{qb}")
                    p.wait_free("tensor", tag + f"e1{qb}")
                for head in (0, 1):
                    ps_ = psS[head][kt % 2]
                    lhs = q2[head * 64:head * 64 + 64, qt * 128:(qt + 1) * 128]
                    rhs = k2[head * 64:head * 64 + 64, kt * 512:(kt + 1) * 512]
                    p.op("tensor", lambda e, l=lhs, r=rhs, b=ps_: e.matmul(
                        b[:, :], l, r, start=True, stop=True),
                        inc="pe" if head == 1 else None)
                p.wait("scalar", "pe")
                for head, e_ in ((0, eA[qb]), (1, eB[qb])):
                    ps_ = psS[head][kt % 2]
                    p.op("scalar", lambda e, b=ps_, e_=e_, kt=kt, h=head: e.activation(
                        e_[:, kt * 512:(kt + 1) * 512], b[:, :], AF.Exp,
                        bias=nbt[:, h:h + 1], scale=0.125), inc="act")
                p.set_free(tag + f"psS0{kt % 2}", "act")
                p.set_free(tag + f"psS1{kt % 2}", "act")

        def mask_tr0(qt, exp_done):
            qg = qt // 4
            qtm = qt % 4
            qb = qt % 2
            p.wait("vector", "act", exp_done)
            for e_ in (eA[qb], eB[qb]):
                p.op("vector", lambda e, e_=e_, qg=qg, qtm=qtm: e.tensor_tensor(
                    e_[:, qg * 512:(qg + 1) * 512], e_[:, qg * 512:(qg + 1) * 512],
                    cst["mask_bf"][:, qtm * 512:(qtm + 1) * 512], ALU.mult), inc="dve")
            mask_done = p.sc["dve"]
            nkc = 4 * qg + 4
            for head, e_, pt_ in ((0, eA[qb], ptA), (1, eB[qb], ptB)):
                for kcg in range((nkc + 7) // 8):
                    bank = ptp[kcg % 2]
                    j0 = kcg * 8
                    j1 = min(j0 + 8, nkc)
                    p.wait("tensor", "dve", mask_done)
                    p.wait_free("tensor", tag + f"ptp{kcg % 2}")
                    p.wait_free("tensor", tag + "ptAB")
                    for j in range(j0, j1):
                        p.op("tensor", lambda e, e_=e_, b=bank, j=j, j0=j0: e.transpose(
                            b[:, (j - j0) * 128:(j - j0 + 1) * 128],
                            e_[:, j * 128:(j + 1) * 128], cst["ident_bf"][:, :]),
                            inc="pe" if j == j1 - 1 else None)
                    p.wait("scalar", "pe")
                    nj = j1 - j0
                    dsl = pt_[:, qtm * S + j0 * 128: qtm * S + j1 * 128]
                    p.op("scalar", lambda e, b=bank, d=dsl, nj=nj: e.activation(
                        d, b[:, 0:nj * 128], AF.Copy), inc="act")
                    p.set_free(tag + f"ptp{kcg % 2}", "act")
                p.set_free(tag + f"e{head}{qb}", "pe")

        def av0(qg):
            pt_done = p.sc["act"]
            nkc = 4 * qg + 4
            for head, pt_, pv_ in ((0, ptA, pvv[0]), (1, ptB, pvv[1])):
                p.wait("tensor", "act", pt_done)
                p.wait_free("tensor", tag + f"pv{head}")
                pt3 = pt_[:, :].rearrange("p (m n) -> p m n", n=128)
                for kc in range(nkc):
                    lhs = vte[:, head * 2080 + kc * 65: head * 2080 + (kc + 1) * 65]
                    rhs = pt3[:, kc::32, :]
                    p.op("tensor", lambda e, l=lhs, r=rhs, b=pv_, s=(kc == 0),
                         t=(kc == nkc - 1): e.matmul(b[:, :], l, r, start=s, stop=t),
                         inc="pe" if kc == nkc - 1 else None)
            p.set_free(tag + "ptAB", "pe")
            av_done = p.sc["pe"]
            p.wait("scalar", "pe", av_done)
            p.op("scalar", lambda e: e.activation(rowr[:, :], pvv[0][64:65, :], AF.Ln),
                 inc="act")
            p.op("scalar", lambda e: e.activation(rowr2[:, :], rowr[:, :], AF.Exp,
                                                  scale=-1.0), inc="act")
            p.wait("tensor", "act")
            p.wait_free("tensor", tag + "pmis")
            p.wait_free("tensor", tag + "psS00")
            p.op("tensor", lambda e: e.matmul(pmis[:, :], cst["onesA_f"][:, :],
                                              rowr2[:, :], start=True, stop=False), inc="pe")
            p.op("scalar", lambda e: e.activation(rowr[:, :], pvv[1][64:65, :], AF.Ln),
                 inc="act")
            p.wait("scalar", "pe")
            p.op("scalar", lambda e: e.activation(rowr2[:, :], rowr[:, :], AF.Exp,
                                                  scale=-1.0), inc="act")
            p.wait("tensor", "act")
            p.op("tensor", lambda e: e.matmul(pmis[:, :], cst["onesB_f"][:, :],
                                              rowr2[:, :], start=False, stop=True), inc="pe")
            p.wait("scalar", "pe")
            p.wait("scalar", "dve")
            p.op("scalar", lambda e: e.activation(rbt[:, :], pmis[:, :], AF.Copy), inc="act")
            p.set_free(tag + "pmis", "act"); p.set_free(tag + "psS00", "act")
            p.op("scalar", lambda e: e.activation(cxt[0:64, :], pvv[0][0:64, :], AF.Copy),
                 inc="act")
            p.op("scalar", lambda e: e.activation(cxt[64:128, :], pvv[1][0:64, :], AF.Copy),
                 inc="act")
            p.set_free(tag + "pv0", "act")
            p.set_free(tag + "pv1", "act")
            p.wait("vector", "act")
            p.op("vector", lambda e, hp=hp, qg=qg: e.tensor_tensor(
                ctxT[:, hp * S + qg * 512: hp * S + (qg + 1) * 512], cxt[:, :],
                rbt[:, :], ALU.mult), inc="dve")

        exp_dones0 = {}
        for step in range(33):
            if step < 32:
                s_exp0(step)
                exp_dones0[step] = p.sc["act"]
            if step >= 1:
                mask_tr0(step - 1, exp_dones0.pop(step - 1))
            if step >= 4 and step % 4 == 0:
                av0(step // 4 - 1)
        p.set_free(tag + "qkv", "pe")
        p.set_free(tag + "vte", "pe")
        if only_hp0:
            break

    if dbg is not None:
        p.wait("gpsimd", "dve")
        for hp_ in range(C):
            p.dma("gpsimd", dbg[1][:, hp_ * 512:(hp_ + 1) * 512],
                  ctxT[:, hp_ * S: hp_ * S + 512].rearrange("p n -> p n"), "do")
        for engx in ("vector", "scalar", "tensor", "gpsimd", "sync"):
            p.wait(engx, "do")

    for cm in reversed(scope_b):
        cm.__exit__(None, None, None)

    outproj_ln(p, cst, ctxT, src, dst, wo, bor, tag, extra_out=extra_out, dbg=dbg)

    for cm in reversed(scope):
        cm.__exit__(None, None, None)


def outproj_ln(p, cst, ctxT, src, dst, wo, bor, tag, extra_out=None, dbg=None):
    scope = []
    p.section_barrier()

    def sb(name, shape, dtype):
        t, cm = p.sbuf(tag + name, shape, dtype)
        scope.append(cm)
        return t

    def ps(name, shape, dtype=F32):
        t, cm = p.psum(tag + name, shape, dtype)
        scope.append(cm)
        return t

    wob = sb("wob", [128, C * D], BF16)
    bot = sb("bot", [128, C], F32)
    xf = [sb(f"oxf{i}", [128, C * 512], F32) for i in range(2)]
    xb = sb("oxb", [128, C * 512], BF16)
    hb = sb("ohb", [128, C * 512], BF16)
    yac = sb("oyac", [128, C * 512], F32)
    rowa = sb("orowa", [1, 512], F32)
    rowb = sb("orowb", [1, 512], F32)
    po = [ps(f"po{i}", [128, 512]) for i in range(2)]
    pls = ps("opls", [1, 512])
    plb = [ps(f"oplb{i}", [128, 512]) for i in range(2)]

    if bor is not None:
        p.dma("sync", bot[:, :], bor[:, :], "dw")
    for fc in range(C):
        p.wait_free("sync", tag + f"oxf{fc % 2}")
        p.dma("sync", xf[fc % 2][:, 0:D], wo[fc * 128:(fc + 1) * 128, :], "dw")
        p.wait("vector", "dw")
        p.op("vector", lambda e, fc=fc: e.tensor_copy(
            wob[:, fc * D:(fc + 1) * D], xf[fc % 2][:, 0:D]), inc="dve")
        p.set_free(tag + f"oxf{fc % 2}", "dve")
    wcast = p.sc["dve"]

    for tt in range(NT):
        cur = xf[tt % 2]
        p.wait_free("sync", tag + f"oxf{tt % 2}")
        p.dma("sync", cur[:, :].rearrange("p (c n) -> p c n", n=512), x_hbm_tile(src, tt), "dx")
        xld = p.sc["dx"]
        for oc in range(C):
            bank = po[oc % 2]
            p.wait("tensor", "dve", wcast)
            p.wait_free("tensor", tag + f"po{oc % 2}")
            for hp in range(C):
                lhs = wob[:, hp * D + oc * 128: hp * D + (oc + 1) * 128]
                rhs = ctxT[:, hp * S + tt * 512: hp * S + (tt + 1) * 512]
                p.op("tensor", lambda e, l=lhs, r=rhs, b=bank, s=(hp == 0), t=(hp == C - 1):
                     e.matmul(b[:, :], l, r, start=s, stop=t),
                     inc="pe" if hp == C - 1 else None)
            p.wait("scalar", "pe")
            if bor is not None:
                p.op("scalar", lambda e, b=bank, oc=oc: e.activation(
                    yac[:, oc * 512:(oc + 1) * 512], b[:, :], AF.Identity,
                    bias=bot[:, oc:oc + 1], scale=1.0), inc="act")
            else:
                p.op("scalar", lambda e, b=bank, oc=oc: e.activation(
                    yac[:, oc * 512:(oc + 1) * 512], b[:, :], AF.Copy), inc="act")
            p.set_free(tag + f"po{oc % 2}", "act")
        if dbg is not None and tt == 0:
            p.wait("gpsimd", "act")
            p.dma("gpsimd", dbg[0][:, :], yac[:, :].rearrange("p n -> p n"), "do")
            for engx in ("vector", "scalar", "tensor", "gpsimd", "sync"):
                p.wait(engx, "do")
        p.wait("vector", "act")
        p.wait("vector", "dx", xld)
        for oc in range(C):
            p.op("vector", lambda e, oc=oc, cur=cur: e.tensor_tensor(
                yac[:, oc * 512:(oc + 1) * 512], yac[:, oc * 512:(oc + 1) * 512],
                cur[:, oc * 512:(oc + 1) * 512], ALU.add), inc="dve")
        ln_epilogue(p, cst, tag + f"oxf{tt % 2}|" + tag + "o", tt, yac, xb, hb, cur,
                    rowa, rowb, pls, plb, dst, extra_out=extra_out)

    for cm in reversed(scope):
        cm.__exit__(None, None, None)


def build_attn0_only():
    nc = bass.Bass("TRN2", target_bir_lowering=False, debug=False)
    xin = nc.dram_tensor("xin", [D, S], F32, kind="ExternalInput").ap()
    xout = nc.dram_tensor("xout", [D, S], F32, kind="ExternalOutput").ap()
    wq = nc.dram_tensor("wq", [D, D], F32, kind="ExternalInput").ap()
    wk = nc.dram_tensor("wk", [D, D], F32, kind="ExternalInput").ap()
    wv = nc.dram_tensor("wv", [D, D], F32, kind="ExternalInput").ap()
    wo = nc.dram_tensor("wo", [D, D], F32, kind="ExternalInput").ap()
    bor = nc.dram_tensor("bor", [128, C], F32, kind="ExternalInput").ap()
    qs = nc.dram_tensor("qs", [D, S], BF16, kind="Internal").ap()
    ks = nc.dram_tensor("ks", [D, S], BF16, kind="Internal").ap()
    vs = nc.dram_tensor("vs", [D, S], BF16, kind="Internal").ap()
    ctx = contextlib.ExitStack()
    with ctx:
        p = P(nc, ctx.enter_context)
        cst = build_consts(p)
        sec_proj(p, cst, xin, [qs, ks, vs], [wq, wk, wv], "pj0_")
        sec_attn0(p, cst, xin, xout, qs, ks, vs, wo, bor, "a0_")
        p.wait("sync", "do", p.sc["do"])
        emit(nc, p)
    return nc


def build_attn0_debug():
    nc = bass.Bass("TRN2", target_bir_lowering=False, debug=False)
    xin = nc.dram_tensor("xin", [D, S], F32, kind="ExternalInput").ap()
    xout = nc.dram_tensor("xout", [D, S], F32, kind="ExternalOutput").ap()
    wq = nc.dram_tensor("wq", [D, D], F32, kind="ExternalInput").ap()
    wk = nc.dram_tensor("wk", [D, D], F32, kind="ExternalInput").ap()
    wv = nc.dram_tensor("wv", [D, D], F32, kind="ExternalInput").ap()
    wo = nc.dram_tensor("wo", [D, D], F32, kind="ExternalInput").ap()
    bor = nc.dram_tensor("bor", [128, C], F32, kind="ExternalInput").ap()
    qs = nc.dram_tensor("qs", [D, S], BF16, kind="ExternalOutput").ap()
    ks = nc.dram_tensor("ks", [D, S], BF16, kind="Internal").ap()
    vs = nc.dram_tensor("vs", [D, S], BF16, kind="Internal").ap()
    dbg = nc.dram_tensor("dbg", [128, S], F32, kind="ExternalOutput").ap()
    dbg2 = nc.dram_tensor("dbg2", [128, S], F32, kind="ExternalOutput").ap()
    dbg3 = nc.dram_tensor("dbg3", [128, S + 256], F32, kind="ExternalOutput").ap()
    dbg4 = nc.dram_tensor("dbg4", [128, S], F32, kind="ExternalOutput").ap()
    dbg5 = nc.dram_tensor("dbg5", [128, 18], F32, kind="ExternalOutput").ap()
    ctx = contextlib.ExitStack()
    with ctx:
        p = P(nc, ctx.enter_context)
        cst = build_consts(p)
        sec_proj(p, cst, xin, [qs, ks, vs], [wq, wk, wv], "pj0_")
        sec_attn0(p, cst, xin, xout, qs, ks, vs, wo, bor, "a0_", dbg=(dbg, dbg2, dbg3, dbg4, dbg5), only_hp0=False, skip_flash=False)
        p.wait("sync", "do", p.sc["do"])
        emit(nc, p)
    return nc


def sec_attn1(p, cst, src, dst, qs, ks, vs, wo, tag, extra_out=None, dbg=None):
    scope = []

    def sb(name, shape, dtype):
        t, cm = p.sbuf(tag + name, shape, dtype)
        scope.append(cm)
        return t

    ctxT = sb("ctxT", [128, C * S], BF16)

    scope_b = []

    def sbb(name, shape, dtype):
        t, cm = p.sbuf(tag + name, shape, dtype)
        scope_b.append(cm)
        return t

    def psb(name, shape, dtype=F32):
        t, cm = p.psum(tag + name, shape, dtype)
        scope_b.append(cm)
        return t

    q2 = sbb("q2", [128, S], BF16)
    k2 = sbb("k2", [128, S], BF16)
    v2 = sbb("v2", [128, S], BF16)
    vtok = sbb("vtok", [128, 2 * 2048], BF16)
    eA = [sbb("eA0", [128, S], BF16), sbb("eA1", [128, S], BF16)]
    eB = [sbb("eB0", [128, S], BF16), sbb("eB1", [128, S], BF16)]
    ptA = sbb("ptA", [128, 32 * 512], BF16)
    ptB = sbb("ptB", [128, 32 * 512], BF16)
    sq2 = eA[0]
    rowr = sbb("rowr", [1, 512], F32)
    smf = sbb("smf", [1, 8], F32)
    smf2 = sbb("smf2", [1, 8], F32)
    nbt = sbb("nbt", [128, 2], F32)
    sg = sbb("sg", [128, 64], F32)
    mu = sbb("mu", [128, 64], F32)
    scab = sbb("scab", [128, 128], F32)
    scef = sbb("scef", [128, 128], F32)
    sca = scab[:, 0:64]
    scb = scab[:, 64:128]
    sce = scef[:, 0:64]
    scf = scef[:, 64:128]
    scd = sbb("scd", [128, 64], F32)
    lw = sbb("lw", [128, 64], F32)
    wbf = sbb("wbf", [128, 64], BF16)
    psS = [[psb("psSA0", [128, 512]), psb("psSA1", [128, 512])],
           [psb("psSB0", [128, 512]), psb("psSB1", [128, 512])]]
    pvv = [psb("pvA", [64, 512]), psb("pvB", [64, 512])]
    pmis = psS[0][0]
    ptp = [psb(f"ptp{i}", [128, 1024], BF16) for i in range(2)]

    p.section_barrier()
    p.wait_free("sync", "bootmem")
    for hp in range(C):
        p.wait_free("sync", tag + "qkv")
        for (t_, s_) in ((q2, qs), (k2, ks), (v2, vs)):
            p.dma("sync", t_[:, :], s_[hp * 128:(hp + 1) * 128, :], "dx")
        ld_qkv = p.sc["dx"]

        for head in range(2):
            ide = cst["ident_bf"][head * 64:head * 64 + 64, head * 64:head * 64 + 64]
            for kcg in range(4):
                bank = ptp[kcg % 2]
                p.wait("tensor", "dx", ld_qkv)
                p.wait_free("tensor", tag + f"ptp{kcg % 2}")
                for j in range(8):
                    kc = kcg * 8 + j
                    vsl = v2[head * 64:head * 64 + 64, kc * 128:(kc + 1) * 128]
                    p.op("tensor", lambda e, v=vsl, b=bank, j=j, ide=ide: e.transpose(
                        b[:, j * 64:(j + 1) * 64], v, ide),
                        inc="pe" if j == 7 else None)
                p.wait("vector", "pe")
                p.wait_free("vector", tag + "vtok")
                dstv = vtok[:, head * 2048 + kcg * 512: head * 2048 + (kcg + 1) * 512]
                p.op("vector", lambda e, b=bank, d=dstv: e.tensor_copy(
                    d, b[:, 0:512]), inc="dve")
                p.set_free(tag + f"ptp{kcg % 2}", "dve")

        for which, t_ in ((0, q2), (1, k2)):
            p.wait("scalar", "dx", ld_qkv)
            p.wait("scalar", "pe")
            p.op("scalar", lambda e, t_=t_: e.activation(sq2[:, :], t_[:, :], AF.Square),
                 inc="act")
            sqd = p.sc["act"]
            for head in range(2):
                onescol = cst["onesA_bf"] if head == 0 else cst["onesB_bf"]
                for tt in range(NT):
                    p.wait("tensor", "act", sqd)
                    p.wait_free("tensor", tag + "pmis")
                    p.op("tensor", lambda e, o=onescol, tt=tt: e.matmul(
                        pmis[0:1, :], o[:, :], sq2[:, tt * 512:(tt + 1) * 512],
                        start=True, stop=True), inc="pe")
                    p.wait("vector", "pe")
                    p.op("vector", lambda e, tt=tt: e.reduce_max(
                        smf2[:, tt:tt + 1], pmis[0:1, :], axis=AX.X), inc="dve")
                    p.set_free(tag + "pmis", "dve"); p.set_free(tag + "psS00", "dve")
                p.wait("vector", "dve")
                p.op("vector", lambda e, i=which * 2 + head: e.reduce_max(
                    smf[:, i:i + 1], smf2[:, :], axis=AX.X), inc="dve")
        for head in range(2):
            p.wait("vector", "dve")
            p.op("vector", lambda e, h=head: e.tensor_tensor(
                smf[:, 4 + h:5 + h], smf[:, h:h + 1], smf[:, 2 + h:3 + h], ALU.mult),
                inc="dve")
            p.wait("scalar", "dve")
            p.op("scalar", lambda e, h=head: e.activation(
                smf[:, 6 + h:7 + h], smf[:, 4 + h:5 + h], AF.Sqrt), inc="act")
            p.wait("vector", "act")
            p.op("vector", lambda e, h=head: e.tensor_scalar(
                smf[:, 4 + h:5 + h], smf[:, 6 + h:7 + h], -0.125, None, ALU.mult),
                inc="dve")
            p.wait("tensor", "dve")
            p.wait_free("tensor", tag + "pmis")
            p.op("tensor", lambda e, h=head: e.matmul(
                pmis[:, 0:1], cst["ones_row_f"][:, :], smf[:, 4 + h:5 + h],
                start=True, stop=True), inc="pe")
            p.wait("scalar", "pe")
            p.op("scalar", lambda e, h=head: e.activation(
                nbt[:, h:h + 1], pmis[:, 0:1], AF.Copy), inc="act")
            p.set_free(tag + "pmis", "act"); p.set_free(tag + "psS00", "act")
        nb_done = p.sc["act"]

        def s_exp(qt):
            qb = qt % 2
            for kt in range(NT):
                kb = kt % 2
                w = max(nb_done,
                        p.free_at.get(tag + f"psS0{kb}", ("act", 0))[1],
                        p.free_at.get(tag + f"psS1{kb}", ("act", 0))[1])
                p.wait("tensor", "act", w)
                p.wait_free("tensor", tag + "pmis")
                if kt == 0:
                    p.wait_free("tensor", tag + f"e0{qb}")
                    p.wait_free("tensor", tag + f"e1{qb}")
                for head, ps_ in ((0, psS[0][kb]), (1, psS[1][kb])):
                    lhs = q2[head * 64:head * 64 + 64, qt * 128:(qt + 1) * 128]
                    rhs = k2[head * 64:head * 64 + 64, kt * 512:(kt + 1) * 512]
                    p.op("tensor", lambda e, l=lhs, r=rhs, b=ps_: e.matmul(
                        b[:, :], l, r, start=True, stop=True),
                        inc="pe" if head == 1 else None)
                p.wait("scalar", "pe")
                for head, e_, ps_ in ((0, eA[qb], psS[0][kb]), (1, eB[qb], psS[1][kb])):
                    p.op("scalar", lambda e, b=ps_, e_=e_, kt=kt, h=head: e.activation(
                        e_[:, kt * 512:(kt + 1) * 512], b[:, :], AF.Exp,
                        bias=nbt[:, h:h + 1], scale=0.125), inc="act")
                p.set_free(tag + f"psS0{kb}", "act")
                p.set_free(tag + f"psS1{kb}", "act")
                if kb == 0:
                    p.set_free(tag + "pmis", "act")

        def stats_part(qt, exp_done):
            qb = qt % 2
            qtm = qt % 4
            for head, e_, pt_ in ((0, eA[qb], ptA), (1, eB[qb], ptB)):
                e3 = e_[:, :].rearrange("p (b w) -> p b w", w=BKV)
                p.wait("vector", "act", exp_done)
                p.op("vector", lambda e, e3=e3: e.tensor_reduce(
                    sg[:, :], e3, AX.X, ALU.add), inc="dve")
                p.op("vector", lambda e, e3=e3: e.tensor_reduce(
                    mu[:, :], e3, AX.X, ALU.max), inc="dve")
                p.wait("vector", "dve")
                p.op("vector", lambda e: e.tensor_tensor_scan(
                    sca, mu[:, :], mu[:, :], 0.0, ALU.max, ALU.bypass),
                    inc="dve")
                p.op("vector", lambda e: e.tensor_tensor_scan(
                    scb, sg[:, :], sg[:, :], 0.0, ALU.add, ALU.bypass),
                    inc="dve")
                p.wait("scalar", "dve")
                p.op("scalar", lambda e: e.activation(scef[:, :], scab[:, :], AF.Ln),
                     inc="act")
                p.wait("vector", "act")
                p.op("vector", lambda e: e.tensor_tensor_scan(
                    sca, sce, sce, 0.0, ALU.add, ALU.bypass),
                    inc="dve")
                p.op("vector", lambda e: e.tensor_tensor_scan(
                    scb, scf, scf, 0.0, ALU.add, ALU.bypass),
                    inc="dve")
                p.wait("vector", "dve")
                p.op("vector", lambda e: e.tensor_tensor(
                    scd[:, :], scb, sca, ALU.subtract), inc="dve")
                p.op("vector", lambda e: e.tensor_tensor(
                    lw[:, 0:1], scab[:, 62:63], scab[:, 64 + 63:64 + 64], ALU.subtract), inc="dve")
                p.wait("vector", "dve")
                p.op("vector", lambda e: e.tensor_scalar(
                    lw[:, 1:64], scd[:, 0:63], lw[:, 0:1], None, ALU.add), inc="dve")
                p.wait("scalar", "dve")
                p.op("scalar", lambda e: e.activation(wbf[:, :], lw[:, :], AF.Exp),
                     inc="act")
                p.wait("vector", "act")
                sd = []
                for sg_ in range(4):
                    wb3 = wbf[:, sg_ * 16:(sg_ + 1) * 16].unsqueeze(2).broadcast_to(
                        [128, 16, BKV])
                    e3s = e_[:, sg_ * 1024:(sg_ + 1) * 1024].rearrange(
                        "p (b w) -> p b w", w=BKV)
                    p.op("vector", lambda e, e3s=e3s, wb3=wb3: e.tensor_tensor(
                        e3s, e3s, wb3, ALU.mult), inc="dve")
                    sd.append(p.sc["dve"])
                scale_dones[(qt, head)] = sd

        def tr_part(qt):
            qb = qt % 2
            qtm = qt % 4
            for head, e_, pt_ in ((0, eA[qb], ptA), (1, eB[qb], ptB)):
                sd = scale_dones.pop((qt, head))
                for kcg in range(4):
                    bank = ptp[kcg % 2]
                    p.wait("tensor", "dve", sd[kcg])
                    p.wait_free("tensor", tag + f"ptp{kcg % 2}")
                    p.wait_free("tensor", tag + "ptAB")
                    for j in range(8):
                        kc = kcg * 8 + j
                        p.op("tensor", lambda e, p_=e_, b=bank, j=j, kc=kc: e.transpose(
                            b[:, j * 128:(j + 1) * 128],
                            p_[:, kc * 128:(kc + 1) * 128], cst["ident_bf"][:, :]),
                            inc="pe" if j == 7 else None)
                    p.wait("scalar", "pe")
                    dsl = pt_[:, qtm * S + kcg * 1024: qtm * S + (kcg + 1) * 1024]
                    p.op("scalar", lambda e, b=bank, d=dsl: e.activation(
                        d, b[:, :], AF.Copy), inc="act")
                    p.set_free(tag + f"ptp{kcg % 2}", "act")
                p.set_free(tag + f"e{head}{qb}", "pe")

        def av(qg):
            pt_done = p.sc["act"]
            for head, pt_, pv_ in ((0, ptA, pvv[0]), (1, ptB, pvv[1])):
                p.wait("tensor", "act", pt_done)
                p.wait_free("tensor", tag + f"pv{head}")
                pt3 = pt_[:, :].rearrange("p (m n) -> p m n", n=128)
                for kc in range(32):
                    lhs = vtok[:, head * 2048 + kc * 64: head * 2048 + (kc + 1) * 64]
                    rhs = pt3[:, kc::32, :]
                    p.op("tensor", lambda e, l=lhs, r=rhs, b=pv_, s=(kc == 0),
                         t=(kc == 31): e.matmul(b[:, :], l, r, start=s, stop=t),
                         inc="pe" if kc == 31 else None)
            p.set_free(tag + "ptAB", "pe")
            p.wait("scalar", "pe")
            p.op("scalar", lambda e, hp=hp, qg=qg: e.activation(
                ctxT[0:64, hp * S + qg * 512: hp * S + (qg + 1) * 512],
                pvv[0][:, :], AF.Copy), inc="act")
            p.op("scalar", lambda e, hp=hp, qg=qg: e.activation(
                ctxT[64:128, hp * S + qg * 512: hp * S + (qg + 1) * 512],
                pvv[1][:, :], AF.Copy), inc="act")
            p.set_free(tag + "pv0", "act")
            p.set_free(tag + "pv1", "act")

        exp_dones = {}
        scale_dones = {}
        for step in range(33):
            if step >= 1:
                stats_part(step - 1, exp_dones.pop(step - 1))
            if step < 32:
                s_exp(step)
                exp_dones[step] = p.sc["act"]
            if step >= 1:
                tr_part(step - 1)
            if step >= 4 and step % 4 == 0:
                av(step // 4 - 1)
        p.set_free(tag + "qkv", "pe")
        p.set_free(tag + "vtok", "pe")

    if dbg is not None:
        p.wait("gpsimd", "act")
        for hp_ in range(C):
            p.dma("gpsimd", dbg[1][:, hp_ * 512:(hp_ + 1) * 512],
                  ctxT[:, hp_ * S: hp_ * S + 512].rearrange("p n -> p n"), "do")
        for engx in ("vector", "scalar", "tensor", "gpsimd", "sync"):
            p.wait(engx, "do")

    for cm in reversed(scope_b):
        cm.__exit__(None, None, None)

    outproj_ln(p, cst, ctxT, src, dst, wo, None, tag, extra_out=extra_out)

    for cm in reversed(scope):
        cm.__exit__(None, None, None)


def build_attn1_only():
    nc = bass.Bass("TRN2", target_bir_lowering=False, debug=False)
    xin = nc.dram_tensor("xin", [D, S], F32, kind="ExternalInput").ap()
    xout = nc.dram_tensor("xout", [D, S], F32, kind="ExternalOutput").ap()
    wq = nc.dram_tensor("wq", [D, D], F32, kind="ExternalInput").ap()
    wk = nc.dram_tensor("wk", [D, D], F32, kind="ExternalInput").ap()
    wv = nc.dram_tensor("wv", [D, D], F32, kind="ExternalInput").ap()
    wo = nc.dram_tensor("wo", [D, D], F32, kind="ExternalInput").ap()
    qs = nc.dram_tensor("qs", [D, S], BF16, kind="Internal").ap()
    ks = nc.dram_tensor("ks", [D, S], BF16, kind="Internal").ap()
    vs = nc.dram_tensor("vs", [D, S], BF16, kind="Internal").ap()
    ctx = contextlib.ExitStack()
    with ctx:
        p = P(nc, ctx.enter_context)
        cst = build_consts(p)
        sec_proj(p, cst, xin, [qs, ks, vs], [wq, wk, wv], "pj1_")
        sec_attn1(p, cst, xin, xout, qs, ks, vs, wo, "a1_")
        p.wait("sync", "do", p.sc["do"])
        emit(nc, p)
    return nc


def build_attn1_debug():
    nc = bass.Bass("TRN2", target_bir_lowering=False, debug=False)
    xin = nc.dram_tensor("xin", [D, S], F32, kind="ExternalInput").ap()
    xout = nc.dram_tensor("xout", [D, S], F32, kind="ExternalOutput").ap()
    wq = nc.dram_tensor("wq", [D, D], F32, kind="ExternalInput").ap()
    wk = nc.dram_tensor("wk", [D, D], F32, kind="ExternalInput").ap()
    wv = nc.dram_tensor("wv", [D, D], F32, kind="ExternalInput").ap()
    wo = nc.dram_tensor("wo", [D, D], F32, kind="ExternalInput").ap()
    qs = nc.dram_tensor("qs", [D, S], BF16, kind="Internal").ap()
    ks = nc.dram_tensor("ks", [D, S], BF16, kind="Internal").ap()
    vs = nc.dram_tensor("vs", [D, S], BF16, kind="Internal").ap()
    dbg = nc.dram_tensor("dbg", [128, S], F32, kind="ExternalOutput").ap()
    dbg2 = nc.dram_tensor("dbg2", [128, S], F32, kind="ExternalOutput").ap()
    dbg3 = nc.dram_tensor("dbg3", [128, 256], F32, kind="ExternalOutput").ap()
    dbg4 = nc.dram_tensor("dbg4", [128, S], F32, kind="ExternalOutput").ap()
    dbg6 = nc.dram_tensor("dbg6", [128, 256], F32, kind="ExternalOutput").ap()
    ctx = contextlib.ExitStack()
    with ctx:
        p = P(nc, ctx.enter_context)
        cst = build_consts(p)
        sec_proj(p, cst, xin, [qs, ks, vs], [wq, wk, wv], "pj1_")
        sec_attn1(p, cst, xin, xout, qs, ks, vs, wo, "a1_", dbg=(dbg, dbg2, dbg3, dbg4, dbg6))
        p.wait("sync", "do", p.sc["do"])
        emit(nc, p)
    return nc


def sec_moe(p, cst, src, dst, wgT, W1, b1all, W2, b2all, acc, tag, extra_out=None, dbg=None):
    nc = p.nc
    cws = nc.dram_tensor(tag + "cws", [8, S], BF16, kind="Internal").ap()
    scope = []

    def sb(name, shape, dtype):
        t, cm = p.sbuf(tag + name, shape, dtype)
        scope.append(cm)
        return t

    gsco = []

    def sbg(name, shape, dtype):
        t, cm = p.sbuf(tag + name, shape, dtype)
        gsco.append(cm)
        return t

    def psg(name, shape, dtype=F32):
        t, cm = p.psum(tag + name, shape, dtype)
        gsco.append(cm)
        return t

    p.section_barrier()
    cwrow = sbg("cwrow", [8, S], BF16)
    wgb = sbg("wgb", [128, C * 8], BF16)
    wgs = sbg("wgs", [128, C * 8], F32)
    xfg = sbg("xfg", [128, C * 512], F32)
    xbg = sbg("xbg", [128, C * 512], BF16)
    gts = sbg("gts", [8, 512], F32)
    gtok = sbg("gtok", [128, 8], F32)
    gt1 = sbg("gt1", [128, 8], F32)
    gt2 = sbg("gt2", [128, 8], F32)
    gtb = sbg("gtb", [128, 8], BF16)
    gm1 = sbg("gm1", [128, 1], F32)
    gm2 = sbg("gm2", [128, 1], F32)
    gm3 = sbg("gm3", [128, 1], F32)
    gm4 = sbg("gm4", [128, 1], F32)
    gt3 = sbg("gt3", [128, 8], F32)
    pg = psg("pg", [8, 512])
    pmg = psg("pmg", [128, 512])
    pmgb = psg("pmgb", [128, 1024], BF16)

    p.dma("sync", wgs[:, :], wgT[:, :], "dw")
    p.wait("vector", "dw")
    p.op("vector", lambda e: e.tensor_copy(wgb[:, :], wgs[:, :]), inc="dve")
    wg_done = p.sc["dve"]
    for tt in range(NT):
        p.wait_free("sync", tag + "xfg")
        p.dma("sync", xfg[:, :].rearrange("p (c n) -> p c n", n=512), x_hbm_tile(src, tt), "dx")
        p.wait("vector", "dx")
        p.op("vector", lambda e: e.tensor_copy(xbg[:, :], xfg[:, :]), inc="dve")
        p.set_free(tag + "xfg", "dve")
        gcast = p.sc["dve"]
        for c in range(4):
            p.wait("tensor", "dve", gcast)
            p.wait_free("tensor", tag + "pmg")
            for fc in range(C):
                lhs = xbg[:, fc * 512 + c * 128: fc * 512 + (c + 1) * 128]
                p.op("tensor", lambda e, l=lhs, fc=fc, s=(fc == 0), t=(fc == C - 1):
                     e.matmul(pmg[0:128, 0:8], l, wgb[:, fc * 8:(fc + 1) * 8],
                              start=s, stop=t), inc="pe" if fc == C - 1 else None)
            p.wait("vector", "pe")
            p.op("vector", lambda e: e.tensor_copy(gtok[:, :], pmg[0:128, 0:8]), inc="dve")
            p.set_free(tag + "pmg", "dve")
            p.wait("vector", "dve")
            p.op("vector", lambda e: e.reduce_max(gm1[:, :], gtok[:, :], axis=AX.X,
                                                  negate=True), inc="dve")
            p.wait("scalar", "dve")
            p.op("scalar", lambda e: e.activation(gt1[:, :], gtok[:, :], AF.Exp,
                                                  bias=gm1[:, :], scale=1.0), inc="act")
            p.wait("vector", "act")
            p.op("vector", lambda e: e.reduce_sum(gm2[:, :], gt1[:, :], axis=AX.X),
                 inc="dve")
            p.wait("vector", "dve")
            p.op("vector", lambda e: e.reciprocal(gm3[:, :], gm2[:, :]), inc="dve")
            p.op("vector", lambda e: e.tensor_scalar(gm4[:, :], gm1[:, :], -1.0,
                                                     None, ALU.mult), inc="dve")
            p.wait("vector", "dve")
            p.op("vector", lambda e: e.tensor_scalar(gt3[:, :], gt1[:, :], gm3[:, :],
                                                     None, ALU.mult), inc="dve")
            p.op("vector", lambda e: e.tensor_scalar(gt2[:, :], gtok[:, :], gm4[:, :],
                                                     None, ALU.is_ge), inc="dve")
            p.wait("vector", "dve")
            if dbg is not None and tt == 7 and c == 3:
                p.wait("gpsimd", "dve")
                p.dma("gpsimd", dbg[1][:, 2000:2008], gt2[:, :], "do")
                p.dma("gpsimd", dbg[1][:, 2008:2016], gt1[:, :], "do")
                for engx in ("vector", "scalar", "tensor", "gpsimd", "sync"):
                    p.wait(engx, "do")
            p.op("vector", lambda e: e.tensor_scalar(gt2[:, :], gt2[:, :], 1e30,
                                                     None, ALU.mult), inc="dve")
            p.wait("vector", "dve")
            p.op("vector", lambda e: e.tensor_tensor(gt2[:, :], gtok[:, :], gt2[:, :],
                                                     ALU.subtract), inc="dve")
            p.wait("vector", "dve")
            p.op("vector", lambda e: e.reduce_max(gm2[:, :], gt2[:, :], axis=AX.X),
                 inc="dve")
            p.wait("vector", "dve")
            p.op("vector", lambda e: e.tensor_scalar(gt2[:, :], gtok[:, :], gm2[:, :],
                                                     None, ALU.is_ge), inc="dve")
            p.wait("vector", "dve")
            p.op("vector", lambda e: e.tensor_tensor(gt1[:, :], gt3[:, :], gt2[:, :],
                                                     ALU.mult), inc="dve")
            p.wait("vector", "dve")
            if dbg is not None and tt == 7 and c == 3:
                p.wait("gpsimd", "dve")
                p.dma("gpsimd", dbg[1][:, 3100:3164], wgs[:, :], "do")
                p.dma("gpsimd", dbg[1][:, 3200:3712], xbg[:, 0:512], "do")
                p.dma("gpsimd", dbg[1][:, 3000:3008], gtok[:, :], "do")
                p.dma("gpsimd", dbg[1][:, 3008:3016], gt1[:, :], "do")
                p.dma("gpsimd", dbg[1][:, 3016:3024], gt2[:, :], "do")
                for engx in ("vector", "scalar", "tensor", "gpsimd", "sync"):
                    p.wait(engx, "do")
            p.op("vector", lambda e: e.tensor_copy(gtb[:, :], gt1[:, :]), inc="dve")
            p.wait("tensor", "dve")
            p.wait_free("tensor", tag + "pmgb")
            p.op("tensor", lambda e: e.transpose(
                pmgb[0:8, 0:128], gtb[:, 0:8], cst["ident_bf"][:, :]), inc="pe")
            p.wait("scalar", "pe")
            p.op("scalar", lambda e, tt=tt, c=c: e.activation(
                cwrow[:, tt * 512 + c * 128: tt * 512 + (c + 1) * 128],
                pmgb[0:8, 0:128], AF.Copy), inc="act")
            p.set_free(tag + "pmgb", "act")
    if dbg is not None:
        p.wait("gpsimd", "act")
        p.dma("gpsimd", dbg[0][0:8, 0:S], cwrow[:, :].rearrange("p n -> p n"), "do")
        for engx in ("vector", "scalar", "tensor", "gpsimd", "sync"):
            p.wait(engx, "do")
    p.wait("gpsimd", "act")
    p.dma("gpsimd", cws[:, :], cwrow[:, :].rearrange("p n -> p n"), "do")
    p.wait("gpsimd", "act")
    p.dma("gpsimd", cws[:, :], cwrow[:, :].rearrange("p n -> p n"), "do")
    for cm in reversed(gsco):
        cm.__exit__(None, None, None)
    cw_done = p.sc["act"]

    esco = []

    def sbe(name, shape, dtype):
        t, cm = p.sbuf(tag + name, shape, dtype)
        esco.append(cm)
        return t

    def pse(name, shape, dtype=F32):
        t, cm = p.psum(tag + name, shape, dtype)
        esco.append(cm)
        return t

    p.section_barrier()
    w1b = sbe("w1b", [128, C * FF], BF16)
    w2b = sbe("w2b", [128, (FF // 128) * D], BF16)
    b1t = sbe("b1t", [128, FF // 128], F32)
    b2t = sbe("b2t", [128, C], F32)
    cwt = sbe("cwt", [8, 1024], BF16)
    xb = sbe("xb", [128, 2 * C * 512], BF16)
    hb = sbe("hb", [128, 2 * 8 * 512], BF16)
    yac = sbe("yac", [128, 2 * C * 512], F32)
    ph = [pse(f"ph{i}", [128, 512]) for i in range(4)]
    py = [pse(f"py{i}", [128, 512]) for i in range(4)]
    pcw = ph[0]

    for ex in range(NE):
        p.dma("sync", b1t[:, :], b1all[:, ex * 32:(ex + 1) * 32], "dw")
        p.dma("sync", b2t[:, :], b2all[:, ex * 8:(ex + 1) * 8], "dw")
        p.wait_free("gpsimd", tag + "w1b")
        for fc in range(C):
            p.dma("gpsimd", w1b[:, fc * FF:(fc + 1) * FF],
                  W1[ex][fc * 128:(fc + 1) * 128, :], "dw")
        p.wait_free("gpsimd", tag + "w2b")
        for g in range(8):
            p.dma("gpsimd", w2b[:, g * 4 * D:(g + 1) * 4 * D].rearrange(
                "p (c n) -> p c n", n=D),
                W2[ex][g * 512:(g + 1) * 512, :].rearrange("(c p) n -> p c n", p=128), "dw")
        wld = p.sc["dw"]

        for tp_ in range(NT // 2):
            t0, t1 = 2 * tp_, 2 * tp_ + 1
            p.wait_free("sync", tag + "cwt")
            p.dma("sync", cwt[:, :], cws[:, t0 * 512:(t0 + 2) * 512], "dx")
            p.wait_free("sync", tag + "xb")
            p.dma("gpsimd", xb[:, 0:C * 512].rearrange("p (c n) -> p c n", n=512),
                  x_hbm_tile(src, t0), "dx")
            p.dma("gpsimd", xb[:, C * 512:2 * C * 512].rearrange("p (c n) -> p c n", n=512),
                  x_hbm_tile(src, t1), "dx")
            xld = p.sc["dx"]
            for g in range(4):
                for hc in range(8):
                    ghc = g * 8 + hc
                    p.wait("tensor", "dx", xld)
                    p.wait("tensor", "dw", wld)
                    p.wait_free("tensor", tag + f"ph0{hc % 2}")
                    p.wait_free("tensor", tag + f"ph1{hc % 2}")
                    if hc % 2 == 0:
                        p.wait_free("tensor", tag + "pcw")
                    for fc in range(C):
                        lhs = w1b[:, fc * FF + ghc * 128: fc * FF + (ghc + 1) * 128]
                        for ti, bank in ((0, ph[hc % 2]), (1, ph[2 + hc % 2])):
                            rhs = xb[:, ti * C * 512 + fc * 512: ti * C * 512 + (fc + 1) * 512]
                            p.op("tensor", lambda e, l=lhs, r=rhs, b=bank, s=(fc == 0),
                                 t=(fc == C - 1): e.matmul(b[:, :], l, r, start=s, stop=t),
                                 inc="pe" if (fc == C - 1 and ti == 1) else None)
                    p.wait("scalar", "pe")
                    for ti, bank in ((0, ph[hc % 2]), (1, ph[2 + hc % 2])):
                        p.op("scalar", lambda e, ghc=ghc, hc=hc, b=bank, ti=ti: e.activation(
                            hb[:, ti * 8 * 512 + hc * 512: ti * 8 * 512 + (hc + 1) * 512],
                            b[:, :], AF.Gelu, bias=b1t[:, ghc:ghc + 1], scale=1.0),
                            inc="act")
                        p.set_free(tag + f"ph{ti}{hc % 2}", "act")
                gelu_done = p.sc["act"]
                for ec in range(C):
                    p.wait("tensor", "act", gelu_done)
                    p.wait_free("tensor", tag + f"py0{ec % 2}")
                    p.wait_free("tensor", tag + f"py1{ec % 2}")
                    for hc in range(8):
                        ghc = g * 8 + hc
                        lhs = w2b[:, ghc * D + ec * 128: ghc * D + (ec + 1) * 128]
                        for ti, bank in ((0, py[ec % 2]), (1, py[2 + ec % 2])):
                            rhs = hb[:, ti * 8 * 512 + hc * 512: ti * 8 * 512 + (hc + 1) * 512]
                            p.op("tensor", lambda e, l=lhs, r=rhs, b=bank, s=(hc == 0),
                                 t=(hc == 7): e.matmul(b[:, :], l, r, start=s, stop=t),
                                 inc="pe" if (hc == 7 and ti == 1) else None)
                    for ti in (0, 1):
                        bank = py[2 * ti + ec % 2] if False else (py[ec % 2] if ti == 0 else py[2 + ec % 2])
                        yslice = yac[:, ti * C * 512 + ec * 512: ti * C * 512 + (ec + 1) * 512]
                        if g == 0:
                            p.wait("scalar", "pe")
                            p.wait("scalar", "do")
                            p.op("scalar", lambda e, y=yslice, b=bank, ec=ec: e.activation(
                                y, b[:, :], AF.Identity, bias=b2t[:, ec:ec + 1],
                                scale=1.0), inc="act")
                            p.set_free(tag + f"py{ti}{ec % 2}", "act")
                        else:
                            p.wait("vector", "pe")
                            p.op("vector", lambda e, y=yslice, b=bank: e.tensor_tensor(
                                y, y, b[:, :], ALU.add), inc="dve")
                            p.set_free(tag + f"py{ti}{ec % 2}", "dve")
            for ti, tt in ((0, t0), (1, t1)):
                p.wait("tensor", "dx", xld)
                p.wait_free("tensor", tag + "pcw")
                p.op("tensor", lambda e, ex=ex, ti=ti: e.matmul(
                    pcw[:, :], cst["sel8"][0:8, ex * 128:(ex + 1) * 128],
                    cwt[0:8, ti * 512:(ti + 1) * 512], start=True, stop=True), inc="pe")
                p.wait("vector", "pe")
                p.wait("vector", "act")
                for oc in range(C):
                    p.op("vector", lambda e, oc=oc, ti=ti: e.tensor_tensor(
                        yac[:, ti * C * 512 + oc * 512: ti * C * 512 + (oc + 1) * 512],
                        yac[:, ti * C * 512 + oc * 512: ti * C * 512 + (oc + 1) * 512],
                        pcw[:, :], ALU.mult), inc="dve")
                p.set_free(tag + "pcw", "dve"); p.set_free(tag + "ph00", "dve")
                p.wait("gpsimd", "dve")
                p.wait_free("gpsimd", tag + f"acc{tt}")
                ysrc = yac[:, ti * C * 512:(ti + 1) * C * 512].rearrange(
                    "p (c n) -> p c n", n=512)
                if ex == 0:
                    p.dma("gpsimd", x_hbm_tile(acc, tt), ysrc, "do")
                else:
                    h, _ = p.inc("do", 16)
                    ot = x_hbm_tile(acc, tt)
                    p.R.op("gpsimd", lambda e, o=ot, h=h, ysrc=ysrc: e.dma_start(
                        out=o, in_=ysrc, accum_op=ALU.add).then_inc(h, 16))
                p.set_free(tag + f"acc{tt}", "do")
            p.set_free(tag + "xb", "do")
            p.set_free(tag + "cwt", "pe")
        p.set_free(tag + "w1b", "pe")
        p.set_free(tag + "w2b", "pe")
    for cm in reversed(esco):
        cm.__exit__(None, None, None)

    fsco = []

    def sbf(name, shape, dtype):
        t, cm = p.sbuf(tag + name, shape, dtype)
        fsco.append(cm)
        return t

    def psf(name, shape, dtype=F32):
        t, cm = p.psum(tag + name, shape, dtype)
        fsco.append(cm)
        return t

    p.section_barrier()
    xf2 = [sbf(f"fxf{i}", [128, C * 512], F32) for i in range(2)]
    ya2 = sbf("fya", [128, C * 512], F32)
    zb2 = sbf("fzb", [128, C * 512], BF16)
    sq2 = sbf("fsq", [128, C * 512], BF16)
    rowa = sbf("frowa", [1, 512], F32)
    rowb = sbf("frowb", [1, 512], F32)
    pls = psf("fpls", [1, 512])
    plb = [psf(f"fplb{i}", [128, 512]) for i in range(2)]
    for tt in range(NT):
        cur = xf2[tt % 2]
        p.wait_free("sync", tag + f"fxf{tt % 2}")
        p.dma("sync", cur[:, :].rearrange("p (c n) -> p c n", n=512), x_hbm_tile(src, tt), "dx")
        p.wait_free("sync", tag + "fya")
        p.dma("sync", ya2[:, :].rearrange("p (c n) -> p c n", n=512), x_hbm_tile(acc, tt), "dx")
        p.wait("vector", "dx")
        for oc in range(C):
            p.op("vector", lambda e, oc=oc, cur=cur: e.tensor_tensor(
                ya2[:, oc * 512:(oc + 1) * 512], ya2[:, oc * 512:(oc + 1) * 512],
                cur[:, oc * 512:(oc + 1) * 512], ALU.add), inc="dve")
        if dbg is not None and len(dbg) > 3 and tt == 0:
            p.wait("gpsimd", "dve")
            p.dma("gpsimd", dbg[3][:, :], ya2[:, :].rearrange("p n -> p n"), "do")
            for engx in ("vector", "scalar", "tensor", "gpsimd", "sync"):
                p.wait(engx, "do")
        ln_epilogue(p, cst, tag + f"fxf{tt % 2}|" + tag + "f", tt, ya2, zb2, sq2, cur,
                    rowa, rowb, pls, plb, dst, extra_out=extra_out,
                    dbgrow=(dbg[2],) if (dbg is not None and len(dbg) > 2 and tt == 0) else None)
        p.set_free(tag + "fya", "dve")
    for cm in reversed(fsco):
        cm.__exit__(None, None, None)
    for cm in reversed(scope):
        cm.__exit__(None, None, None)


def build_moe_only():
    nc = bass.Bass("TRN2", target_bir_lowering=False, debug=False)
    xin = nc.dram_tensor("xin", [D, S], F32, kind="ExternalInput").ap()
    xout = nc.dram_tensor("xout", [D, S], F32, kind="ExternalOutput").ap()
    wgT = nc.dram_tensor("wgT", [128, 64], F32, kind="ExternalInput").ap()
    W1 = [nc.dram_tensor(f"W1_{e}", [D, FF], F32, kind="ExternalInput").ap() for e in range(NE)]
    W2 = [nc.dram_tensor(f"W2_{e}", [FF, D], F32, kind="ExternalInput").ap() for e in range(NE)]
    b1all = nc.dram_tensor("b1all", [128, 32 * NE], F32, kind="ExternalInput").ap()
    b2all = nc.dram_tensor("b2all", [128, 8 * NE], F32, kind="ExternalInput").ap()
    acc = nc.dram_tensor("macc", [D, S], F32, kind="Internal").ap()
    ctx = contextlib.ExitStack()
    with ctx:
        p = P(nc, ctx.enter_context)
        cst = build_consts(p)
        sec_moe(p, cst, xin, xout, wgT, W1, b1all, W2, b2all, acc, "m1_")
        p.wait("sync", "do", p.sc["do"])
        emit(nc, p)
    return nc


def build_moe_debug():
    nc = bass.Bass("TRN2", target_bir_lowering=False, debug=False)
    xin = nc.dram_tensor("xin", [D, S], F32, kind="ExternalInput").ap()
    xout = nc.dram_tensor("xout", [D, S], F32, kind="ExternalOutput").ap()
    wgT = nc.dram_tensor("wgT", [128, 64], F32, kind="ExternalInput").ap()
    W1 = [nc.dram_tensor(f"W1_{e}", [D, FF], F32, kind="ExternalInput").ap() for e in range(NE)]
    W2 = [nc.dram_tensor(f"W2_{e}", [FF, D], F32, kind="ExternalInput").ap() for e in range(NE)]
    b1all = nc.dram_tensor("b1all", [128, 32 * NE], F32, kind="ExternalInput").ap()
    b2all = nc.dram_tensor("b2all", [128, 8 * NE], F32, kind="ExternalInput").ap()
    acc = nc.dram_tensor("macc", [D, S], F32, kind="Internal").ap()
    dbg = nc.dram_tensor("dbg", [128, S], F32, kind="ExternalOutput").ap()
    dbg2 = nc.dram_tensor("dbg2", [128, S], F32, kind="ExternalOutput").ap()
    dbg3 = nc.dram_tensor("dbg3", [128, S], F32, kind="ExternalOutput").ap()
    dbg4 = nc.dram_tensor("dbg4", [128, S], F32, kind="ExternalOutput").ap()
    ctx = contextlib.ExitStack()
    with ctx:
        p = P(nc, ctx.enter_context)
        cst = build_consts(p)
        sec_moe(p, cst, xin, xout, wgT, W1, b1all, W2, b2all, acc, "m1_", dbg=(dbg, dbg2, dbg3, dbg4))
        p.wait("sync", "do", p.sc["do"])
        emit(nc, p)
    return nc
